# version 06 (final)
# speedup vs baseline: 1.1458x; 1.0161x over previous
import contextlib
import math
import numpy as np
import ml_dtypes
import concourse.bass as bass
import concourse.mybir as mybir
from concourse.bass_utils import run_bass_kernel_spmd

F32 = mybir.dt.float32
BF16 = mybir.dt.bfloat16
AF = mybir.ActivationFunctionType
AX = mybir.AxisListType
ALU = mybir.AluOpType

NCORE = 8
B, T, D = 8, 4096, 1024
CH = D // 128
SLAB = 512
NSLAB = T // SLAB
NHEAD_A = 16

ENGINES = ("tensor", "vector", "scalar", "gpsimd", "sync")


class Bld:
    def __init__(self, nc, es):
        self.nc = nc
        self.es = es
        self.q = {e: [] for e in ENGINES}
        self.sems = {}
        self.cnt = {}

    def sem(self, name):
        self.sems[name] = self.es.enter_context(self.nc.semaphore(name))
        self.cnt[name] = 0

    def n(self, name):
        return self.cnt[name]

    CHAIN = {"vector": "V", "scalar": "S"}

    def op(self, eng, fname, *args, w=(), inc=None, k=None, **kw):
        if k is None:
            k = 16 if fname in ("dma_start", "dma_start_transpose", "collective_compute") else 1
        cs = self.CHAIN.get(eng)
        if cs is not None and fname not in ("dma_start", "dma_start_transpose"):
            w = tuple(w) + ((cs, self.cnt[cs]),)
            inc = cs
            k = 1
        post = None
        if inc is not None:
            self.cnt[inc] += k
            post = (self.sems[inc], k)
        waits = [(self.sems[s], v) for (s, v) in w]

        def run(e, fname=fname, args=args, kw=kw, waits=waits, post=post):
            for h, v in waits:
                e.wait_ge(h, v)
            r = getattr(e, fname)(*args, **kw)
            if post is not None:
                r.then_inc(*post)
        self.q[eng].append(run)

    def wait(self, eng, w):
        waits = [(self.sems[s], v) for (s, v) in w]

        def run(e, waits=waits):
            for h, v in waits:
                e.wait_ge(h, v)
        self.q[eng].append(run)

    def raw(self, eng, fn):
        self.q[eng].append(fn)


def host_consts():
    c = {}
    c["ident"] = np.eye(128, dtype=np.float32)
    kr = np.arange(128)[:, None]
    qc = np.arange(64)[None, :]
    c["a_mask_w0"] = (kr >= qc + 65).astype(np.float32)
    qc = np.arange(192)[None, :]
    d_ = qc - kr
    c["a_mask_wm"] = ((d_ >= 0) & (d_ <= 63)).astype(np.float32)
    m = np.zeros((120, 512), np.float32)
    q = np.arange(512)
    for j in range(64):
        m[j] = (q // 64 == j // 8)
    for j in range(32):
        m[64 + j] = (q // 128 == j // 8)
    for j in range(16):
        m[96 + j] = (q // 256 == j // 8)
    m[112:120] = 1.0
    c["a_mask_lvl"] = m
    sv = np.zeros(120, np.float32)
    sv[0:64] = 8.0
    sv[64:96] = 16.0
    sv[96:112] = 32.0
    sv[112:120] = 64.0
    c["a_svec_row"] = sv.reshape(1, 120).copy()
    c["a_svec_col"] = sv.reshape(120, 1).copy()
    sv0 = sv.copy()
    sv0[0:8] = 0.0
    sv0[64:72] = 0.0
    sv0[96:104] = 0.0
    sv0[112:120] = 0.0
    c["a_svec_row0"] = sv0.reshape(1, 120).copy()
    c["a_svec_col0"] = sv0.reshape(120, 1).copy()

    DH = 64
    WINDOW = 512
    n = 4096
    i2 = np.arange(0, DH, 2, dtype=np.float64)
    inv_freq = 1.0 / 10000.0 ** (i2 / DH)
    t = np.arange(n, dtype=np.float64)
    freqs = t[:, None] * inv_freq[None, :]
    base = (i2 + 0.4 * DH) / (1.4 * DH)
    power = (t - n // 2) / float(WINDOW // 2)
    scale = base[None, :] ** power[:, None]
    cos = np.cos(freqs)
    sin = np.sin(freqs)
    AQ = 8.0 * cos * scale
    BQ = 8.0 * sin * scale
    AK = cos / scale
    BK = sin / scale
    def tab128(m):
        half = np.concatenate([m, m], axis=1)
        return np.ascontiguousarray(np.concatenate([half, half], axis=1).T.astype(np.float32))
    c["f_AQ"] = tab128(AQ)
    c["f_BQ"] = tab128(BQ)
    c["f_AK"] = tab128(AK)
    c["f_BK"] = tab128(BK)
    r64 = np.zeros((64, 64), np.float32)
    for dd in range(32):
        r64[dd, dd + 32] = -1.0
        r64[dd + 32, dd] = 1.0
    P = np.zeros((128, 128), np.float32)
    P[0:64, 0:64] = r64.T
    P[64:128, 64:128] = r64.T
    c["f_rot"] = P
    sel = np.zeros((128, 32), np.float32)
    bsel = np.zeros((8, 512), np.float32)
    for cch in range(4):
        sel[0:64, cch * 8 + 2 * cch] = 1.0
        sel[64:128, cch * 8 + 2 * cch + 1] = 1.0
        bsel[2 * cch, cch * 128: cch * 128 + 64] = 1.0
        bsel[2 * cch + 1, cch * 128 + 64: cch * 128 + 128] = 1.0
    c["f_sel"] = sel
    c["f_bsel"] = bsel
    kr = np.arange(128)[:, None]
    u = np.arange(1408)[None, :]
    c["f_maskw"] = ((u >= 384 + kr) & (u <= 896 + kr)).astype(np.float32)
    return c


def host_prep(inputs):
    f32 = np.float32
    consts = host_consts()
    x = np.asarray(inputs["L0_attn_a_x"], f32)
    wa = np.ascontiguousarray(inputs["L0_attn_a_W_attn"], f32)
    ba = np.asarray(inputs["L0_attn_a_b_attn"], f32)
    wp = np.ascontiguousarray(inputs["L0_attn_a_W_proj"], f32)
    bp = np.asarray(inputs["L0_attn_a_b_proj"], f32)

    shared = {
        "wa": wa,
        "wp": wp,
        "ba_qk": np.ascontiguousarray(ba[:2048].reshape(16, 128).T),
        "ba_k_row": np.ascontiguousarray(ba[1024:2048].reshape(1, 1024)),
        "ba_v": np.ascontiguousarray(ba[2048:].reshape(1, 1024)),
        "bp_c": np.ascontiguousarray(bp.reshape(8, 128).T),
    }
    for k, v in consts.items():
        shared[k] = v

    wqkv = np.asarray(inputs["L0_ffn_wqkv"], f32)
    wo = np.asarray(inputs["L0_ffn_wo"], f32)
    w1 = np.asarray(inputs["L0_ffn_w1"], f32)
    b1 = np.asarray(inputs["L0_ffn_b1"], f32)
    w2 = np.asarray(inputs["L0_ffn_w2"], f32)
    b2 = np.asarray(inputs["L0_ffn_b2"], f32)
    lg = np.asarray(inputs["L0_ffn_ln_g"], f32)
    lb = np.asarray(inputs["L0_ffn_ln_b"], f32)
    qsc = np.asarray(inputs["L0_ffn_q_scale"], f32)
    ksc = np.asarray(inputs["L0_ffn_k_scale"], f32)
    HID, HIDP = 2730, 2816
    w1p = np.zeros((2, 1024, 2 * HIDP), f32)
    b1p = np.zeros((2, 2 * HIDP), f32)
    for l in range(2):
        ap = np.zeros((1024, HIDP), f32)
        gp = np.zeros((1024, HIDP), f32)
        ap[:, :HID] = w1[l][:, :HID]
        gp[:, :HID] = w1[l][:, HID:]
        bap = np.zeros(HIDP, f32)
        bgp = np.zeros(HIDP, f32)
        bap[:HID] = b1[l][:HID]
        bgp[:HID] = b1[l][HID:]
        for p in range(22):
            w1p[l][:, 256 * p: 256 * p + 128] = ap[:, 128 * p: 128 * p + 128]
            w1p[l][:, 256 * p + 128: 256 * p + 256] = gp[:, 128 * p: 128 * p + 128]
            b1p[l][256 * p: 256 * p + 128] = bap[128 * p: 128 * p + 128]
            b1p[l][256 * p + 128: 256 * p + 256] = bgp[128 * p: 128 * p + 128]
    w2p = np.zeros((2, HIDP, 1024), f32)
    w2p[:, :HID, :] = w2
    shared["f_wqkv"] = np.ascontiguousarray(wqkv)
    shared["f_wo"] = np.ascontiguousarray(wo)
    shared["f_w1p"] = np.ascontiguousarray(w1p)
    shared["f_w2p"] = np.ascontiguousarray(w2p)
    shared["f_b1p"] = np.ascontiguousarray(b1p.reshape(2, 44, 128).transpose(2, 0, 1).reshape(128, 88))
    shared["f_b2"] = np.ascontiguousarray(b2.reshape(2, 8, 128).transpose(2, 0, 1).reshape(128, 16))
    shared["f_lg"] = np.ascontiguousarray(lg.reshape(2, 8, 128).transpose(2, 0, 1).reshape(128, 16))
    shared["f_lb"] = np.ascontiguousarray(lb.reshape(2, 8, 128).transpose(2, 0, 1).reshape(128, 16))
    shared["f_qs"] = np.ascontiguousarray(np.stack([np.tile(qsc[l], 2) for l in range(2)]))
    shared["f_ks"] = np.ascontiguousarray(np.stack([np.tile(ksc[l], 2) for l in range(2)]))
    maps = []
    for b in range(NCORE):
        m = dict(shared)
        m["xin"] = np.ascontiguousarray(x[b].T)
        maps.append(m)
    return maps


DBG = {"taps": [], "off": 0, "slab": 0}


def _tap(b, io, name, ap, rows, cols, w):
    if io.get("dbg") is None:
        return
    off = DBG["off"]
    b.op("gpsimd", "dma_start", out=io["dbg"][0:rows, off:off + cols], in_=ap, w=w, inc="G",
         allow_slow_non_contiguous=True)
    DBG["taps"].append((name, rows, cols, off))
    DBG["off"] += cols
    g = b.n("G")
    for e in ("tensor", "vector", "scalar", "sync"):
        b.wait(e, (("G", g),))


def emit_attn_a(b: Bld, io, sb, ps):
    nc = b.nc
    PS = ps

    def bank(i, lo=0, n=512):
        return PS[:, 512 * i + lo: 512 * i + lo + n]

    for kc in range(CH):
        b.op("gpsimd", "dma_start", out=sb["wa"][:, kc * 3072:(kc + 1) * 3072],
             in_=io["wa"][kc * 128:(kc + 1) * 128, :], inc="G")
    for kc in range(CH):
        b.op("gpsimd", "dma_start", out=sb["wp"][:, kc * 1024:(kc + 1) * 1024],
             in_=io["wp"][kc * 128:(kc + 1) * 128, :], inc="G")
    b.op("gpsimd", "dma_start", out=sb["ba_qk"][:, :], in_=io["ba_qk"][:, :], inc="G")
    b.op("gpsimd", "dma_start", out=sb["ba_v"][:, :], in_=io["ba_v"][:, :], inc="G")
    b.op("gpsimd", "dma_start", out=sb["bp_c"][:, :], in_=io["bp_c"][:, :], inc="G")
    b.op("gpsimd", "dma_start", out=sb["mask_w0"][:, :], in_=io["a_mask_w0"][:, :], inc="G")
    b.op("gpsimd", "dma_start", out=sb["mask_wm"][:, :], in_=io["a_mask_wm"][:, :], inc="G")
    b.op("gpsimd", "dma_start", out=sb["mask_lvl"][:, :], in_=io["a_mask_lvl"][:, :], inc="G")
    b.op("gpsimd", "dma_start", out=sb["svec_row"][:, :], in_=io["a_svec_row"][:, :], inc="G")
    b.op("gpsimd", "dma_start", out=sb["svec_row0"][:, :], in_=io["a_svec_row0"][:, :], inc="G")
    b.op("gpsimd", "dma_start", out=sb["ba_k_rowb"][:, :], in_=io["ba_k_row"][:, :], inc="G")
    b.op("gpsimd", "dma_start", out=sb["svec_col"][:, :], in_=io["a_svec_col"][:, :], inc="G")
    b.op("gpsimd", "dma_start", out=sb["svec_col0"][:, :], in_=io["a_svec_col0"][:, :], inc="G")
    g_loads = b.n("G")

    b.op("vector", "memset", sb["ones_b"][:, :], 1.0)
    b.op("vector", "memset", sb["ones_f"][:, :], 1.0)
    b.op("vector", "memset", sb["inv1024"][:, :], 1.0 / 1024.0)
    for c in range(CH):
        b.op("vector", "memset", sb["kroll"][:, c * 640: c * 640 + 128], 0.0)
    b.op("vector", "memset", sb["vtr"][:, 0:1024], 0.0)
    b.op("vector", "memset", sb["xptail"][:, :], 0.0)
    b.op("vector", "memset", sb["eps_col"][:, :], 1e-5)
    b.op("vector", "tensor_scalar_mul", sb["bq_s"][:, :], sb["ba_qk"][:, 0:8], 0.125,
         w=(("G", g_loads),), inc="V")
    v_init = b.n("V")

    for s in range(NSLAB):
        qs = s * SLAB
        for c in range(CH):
            b.op("sync", "dma_start", out=sb["x"][:, c * 512:(c + 1) * 512],
                 in_=io["xin"][c * 128:(c + 1) * 128, qs:qs + 512], inc="Y")
        yA = b.n("Y")

        b.wait("scalar", (("Y", yA),))
        for c in range(CH):
            b.op("scalar", "copy", sb["xb"][:, c * 512:(c + 1) * 512], sb["x"][:, c * 512:(c + 1) * 512])
        for c in range(CH):
            b.op("scalar", "activation", sb["xsq"][:, c * 512:(c + 1) * 512],
                 sb["x"][:, c * 512:(c + 1) * 512], AF.Square,
                 inc="S" if c == CH - 1 else None)
        sB = b.n("S")
        b.wait("tensor", (("S", sB), ("V", v_init)))
        for c in range(CH):
            b.op("tensor", "matmul", PS[0:1, 6 * 512:7 * 512], sb["inv1024"][:, :],
                 sb["xb"][:, c * 512:(c + 1) * 512], start=(c == 0), stop=(c == CH - 1))
        for c in range(CH):
            b.op("tensor", "matmul", PS[32:33, 6 * 512:7 * 512], sb["inv1024"][:, :],
                 sb["xsq"][:, c * 512:(c + 1) * 512], start=(c == 0), stop=(c == CH - 1),
                 inc="T" if c == CH - 1 else None)
        tB = b.n("T")
        b.wait("vector", (("T", tB),))
        b.op("vector", "tensor_copy", sb["row_mu"][:, :], PS[0:1, 6 * 512:7 * 512])
        b.op("vector", "tensor_tensor", sb["row_a"][:, :], sb["row_mu"][:, :], sb["row_mu"][:, :], ALU.mult)
        b.op("vector", "tensor_tensor", sb["row_b"][:, :], PS[32:33, 6 * 512:7 * 512], sb["row_a"][:, :],
             ALU.subtract, inc="V")
        vB1 = b.n("V")
        b.wait("scalar", (("V", vB1),))
        b.op("scalar", "activation", sb["row_a"][:, :], sb["row_b"][:, :], AF.Sqrt,
             bias=sb["eps_col"][0:1, :], scale=1.0, inc="S")
        sB2 = b.n("S")
        b.wait("vector", (("S", sB2),))
        b.op("vector", "reciprocal", sb["row_r"][:, :], sb["row_a"][:, :])
        b.op("vector", "scalar_tensor_tensor", sb["row_s"][:, :], sb["row_mu"][:, :], -1.0,
             sb["row_r"][:, :], ALU.mult, ALU.mult, inc="V")
        vB2 = b.n("V")
        b.wait("tensor", (("V", vB2),))
        b.op("tensor", "matmul", bank(6), sb["ones_f"][:, :], sb["row_r"][:, :], start=True, stop=True)
        b.op("tensor", "matmul", bank(7), sb["ones_f"][:, :], sb["row_s"][:, :], start=True, stop=True,
             inc="T")
        tB2 = b.n("T")
        b.wait("vector", (("T", tB2),))
        for c in range(CH):
            b.op("vector", "tensor_tensor", sb["tmpf"][:, 0:512], sb["x"][:, c * 512:(c + 1) * 512],
                 bank(6), ALU.mult)
            b.op("vector", "tensor_tensor", sb["ln1"][:, c * 512:(c + 1) * 512], sb["tmpf"][:, 0:512],
                 bank(7), ALU.add, inc="V" if c == CH - 1 else None)
        vLN = b.n("V")
        if s == DBG["slab"]:
            _tap(b, io, "ln1_c0", sb["ln1"][:, 0:512], 128, 512, (("V", vLN),))

        if s > 0:
            for c in range(CH):
                b.op("vector", "tensor_copy", sb["kroll"][:, c * 640: c * 640 + 128],
                     sb["kroll"][:, c * 640 + 512: c * 640 + 640])
            b.op("vector", "tensor_copy", sb["vtr"][:, 0:1024], sb["vtr"][:, 4 * 1024:5 * 1024],
                 inc="V")
        vRoll = b.n("V")

        b.wait("vector", (("V", vLN),))
        for c in range(CH):
            XO = c * 120
            TO = c * 32
            xps, xpt, xpg = sb["xps"], sb["xptail"], sb["xpg"]
            b.op("vector", "reduce_sum", xps[:, XO:XO + 64],
                 sb["ln1"][:, c * 512:(c + 1) * 512].rearrange("p (n e) -> p n e", e=8), axis=AX.X,
                 inc="V")
            b.wait("vector", (("V", b.n("V")),))
            l1c = xps[:, XO:XO + 64]
            b.op("vector", "tensor_tensor", xps[:, XO + 64:XO + 96],
                 l1c[:, 0:64:2], l1c[:, 1:64:2], ALU.add, inc="V")
            b.wait("vector", (("V", b.n("V")),))
            l2c = xps[:, XO + 64:XO + 96]
            b.op("vector", "tensor_tensor", xps[:, XO + 96:XO + 112],
                 l2c[:, 0:32:2], l2c[:, 1:32:2], ALU.add, inc="V")
            b.wait("vector", (("V", b.n("V")),))
            l3c = xps[:, XO + 96:XO + 112]
            b.op("vector", "tensor_tensor", xps[:, XO + 112:XO + 120],
                 l3c[:, 0:16:2], l3c[:, 1:16:2], ALU.add, inc="V")
            b.wait("vector", (("V", b.n("V")),))
            og = c * 120
            b.op("vector", "tensor_copy", xpg[:, og + 0:og + 8], xpt[:, TO + 0:TO + 8])
            b.op("vector", "tensor_copy", xpg[:, og + 8:og + 64], xps[:, XO + 0:XO + 56])
            b.op("vector", "tensor_copy", xpg[:, og + 64:og + 72], xpt[:, TO + 8:TO + 16])
            b.op("vector", "tensor_copy", xpg[:, og + 72:og + 96], xps[:, XO + 64:XO + 88])
            b.op("vector", "tensor_copy", xpg[:, og + 96:og + 104], xpt[:, TO + 16:TO + 24])
            b.op("vector", "tensor_copy", xpg[:, og + 104:og + 112], xps[:, XO + 96:XO + 104])
            b.op("vector", "tensor_copy", xpg[:, og + 112:og + 120], xpt[:, TO + 24:TO + 32])
            b.op("vector", "tensor_copy", xpt[:, TO + 0:TO + 8], xps[:, XO + 56:XO + 64])
            b.op("vector", "tensor_copy", xpt[:, TO + 8:TO + 16], xps[:, XO + 88:XO + 96])
            b.op("vector", "tensor_copy", xpt[:, TO + 16:TO + 24], xps[:, XO + 104:XO + 112])
            b.op("vector", "tensor_copy", xpt[:, TO + 24:TO + 32], xps[:, XO + 112:XO + 120],
                 inc="V" if c == CH - 1 else None)
        vPool = b.n("V")
        if s == DBG["slab"] or s == DBG["slab"] - 1:
            _tap(b, io, "xps_c0_s%d" % s, sb["xps"][:, 0:120], 128, 120, (("V", vPool),))
            _tap(b, io, "xpt_c0_s%d" % s, sb["xptail"][:, 0:32], 128, 32, (("V", vPool),))
        if s == DBG["slab"]:
            _tap(b, io, "xpg_c0", sb["xpg"][:, 0:120], 128, 120, (("V", vPool),))

        b.wait("tensor", (("V", vRoll), ("V", vPool), ("G", g_loads)))
        a_ev = []
        for mc in range(16):
            bk = mc % 2
            if mc >= 2:
                b.wait("tensor", (("S", a_ev[mc - 2]),))
            for kc in range(CH):
                b.op("tensor", "matmul", bank(bk),
                     sb["wa"][:, kc * 3072 + mc * 128: kc * 3072 + mc * 128 + 128],
                     sb["ln1"][:, kc * 512:(kc + 1) * 512],
                     start=(kc == 0), stop=(kc == CH - 1),
                     inc="T" if kc == CH - 1 else None)
            tE = b.n("T")
            b.wait("scalar", (("T", tE),))
            if mc < 8:
                b.op("scalar", "activation", sb["qf"][:, mc * 512:(mc + 1) * 512], bank(bk),
                     AF.Identity, bias=sb["bq_s"][:, mc:mc + 1], scale=0.125, inc="S")
            else:
                kc2 = mc - 8
                b.op("scalar", "activation", sb["kroll"][:, kc2 * 640 + 128: kc2 * 640 + 640], bank(bk),
                     AF.Identity, bias=sb["ba_qk"][:, mc:mc + 1], scale=1.0, inc="S")
            a_ev.append(b.n("S"))
        sQK = b.n("S")
        if s == DBG["slab"]:
            _tap(b, io, "qf_c0", sb["qf"][:, 0:512], 128, 512, (("S", sQK),))
            _tap(b, io, "kroll_c0", sb["kroll"][:, 0:640], 128, 640, (("S", sQK),))
        for i in range(4):
            for ms in range(2):
                bk = ms
                b.wait("tensor", (("S", sQK),))
                for kc in range(CH):
                    b.op("tensor", "matmul", bank(bk),
                         sb["ln1"][:, kc * 512 + i * 128: kc * 512 + i * 128 + 128],
                         sb["wa"][:, kc * 3072 + 2048 + ms * 512: kc * 3072 + 2048 + ms * 512 + 512],
                         start=(kc == 0), stop=False)
                b.op("tensor", "matmul", bank(bk), sb["ones_b"][0:1, 0:128],
                     sb["ba_vb"][:, ms * 512:(ms + 1) * 512], start=False, stop=True, inc="T")
                tv = b.n("T")
                b.wait("scalar", (("T", tv),))
                b.op("scalar", "copy", sb["vtr"][:, (1 + i) * 1024 + ms * 512: (1 + i) * 1024 + ms * 512 + 512],
                     bank(bk), inc="S")
                sv_ = b.n("S")
                b.wait("tensor", (("S", sv_),))
        sV = b.n("S")
        if s == DBG["slab"]:
            _tap(b, io, "vtr01", sb["vtr"][:, 0:2048], 128, 2048, (("S", sV),))

        svrow = sb["svec_row0"] if s == 0 else sb["svec_row"]
        for mc in range(CH):
            bk = mc % 2
            for kc in range(CH):
                b.op("tensor", "matmul", bank(bk, 0, 120),
                     sb["wa"][:, kc * 3072 + 1024 + mc * 128: kc * 3072 + 1024 + mc * 128 + 128],
                     sb["xpg"][:, kc * 120:(kc + 1) * 120],
                     start=(kc == 0), stop=False)
            b.op("tensor", "matmul", bank(bk, 0, 120),
                 sb["ba_k_rowb"][:, mc * 128:(mc + 1) * 128], svrow[:, :],
                 start=False, stop=True, inc="T")
            tpk = b.n("T")
            b.wait("scalar", (("T", tpk),))
            for (lo, n, sc) in ((0, 64, 1 / 8), (64, 32, 1 / 16), (96, 16, 1 / 32), (112, 8, 1 / 64)):
                b.op("scalar", "activation", sb["pkg"][:, mc * 120 + lo: mc * 120 + lo + n],
                     bank(bk, lo, n), AF.Identity, bias=0.0, scale=float(sc),
                     inc="S" if lo == 112 else None)
            spk = b.n("S")
            if mc + 2 < CH:
                b.wait("tensor", (("S", spk),))
        for ms in range(2):
            bk = ms
            b.wait("tensor", (("S", b.n("S")),))
            for kc in range(CH):
                b.op("tensor", "matmul", bank(bk, 0, 512)[0:120, :],
                     sb["xpg"][:, kc * 120:(kc + 1) * 120],
                     sb["wa"][:, kc * 3072 + 2048 + ms * 512: kc * 3072 + 2048 + ms * 512 + 512],
                     start=(kc == 0), stop=False)
            b.op("tensor", "matmul", bank(bk, 0, 512)[0:120, :], svrow[:, :],
                 sb["ba_vb"][:, ms * 512:(ms + 1) * 512], start=False, stop=True, inc="T")
            tpv = b.n("T")
            b.wait("scalar", (("T", tpv),))
            b.op("scalar", "copy",
                 sb["pvt"][0:120, 520 * ms: 520 * ms + 520].rearrange("p (h d) -> p h d", d=65)[:, :, 0:64],
                 bank(bk, 0, 512)[0:120, :].rearrange("p (h d) -> p h d", d=64),
                 inc="S")
            b.wait("tensor", (("S", b.n("S")),))
        svc = sb["svec_col"]
        for h_ in range(16):
            b.op("vector", "tensor_copy", sb["pvt"][0:120, 65 * h_ + 64: 65 * h_ + 65], svc[:, :])

        if s == DBG["slab"]:
            _tap(b, io, "pkg_c0", sb["pkg"][:, 0:120], 128, 120, (("S", b.n("S")),))
            _tap(b, io, "pvt", sb["pvt"][0:120, 0:1024], 120, 1024, (("S", b.n("S")),))
        SEG_Q = [(0, 64), (0, 192), (128, 192), (256, 192), (384, 128)]
        EXPW_O = [0, 64, 256, 448, 640]
        for h in range(NHEAD_A):
            hp, c_ = h % 2, h // 2
            hp64 = hp * 64
            SWIN_DST = [(2, 0), (2, 64), (2, 256), (3, 0), (3, 192)]
            for cseg in range(5):
                qoff, N = SEG_Q[cseg]
                bkn, off = SWIN_DST[cseg]
                b.op("tensor", "matmul", bank(bkn, off, N),
                     sb["kroll"][hp64:hp64 + 64, c_ * 640 + 128 * cseg: c_ * 640 + 128 * cseg + 128],
                     sb["qf"][hp64:hp64 + 64, c_ * 512 + qoff: c_ * 512 + qoff + N],
                     start=True, stop=True)
            b.op("tensor", "matmul", bank(4, 0, 512)[0:120, :],
                 sb["pkg"][hp64:hp64 + 64, c_ * 120:(c_ + 1) * 120],
                 sb["qf"][hp64:hp64 + 64, c_ * 512:(c_ + 1) * 512],
                 start=True, stop=True, inc="T")
            tS = b.n("T")
            b.wait("scalar", (("T", tS),))
            for cseg in range(5):
                qoff, N = SEG_Q[cseg]
                bkn, off = SWIN_DST[cseg]
                b.op("scalar", "activation", sb["expw"][:, EXPW_O[cseg]:EXPW_O[cseg] + N],
                     bank(bkn, off, N), AF.Exp)
            b.op("scalar", "activation", sb["expl"][0:120, :], bank(4, 0, 512)[0:120, :], AF.Exp,
                 inc="S")
            sE = b.n("S")
            b.wait("vector", (("S", sE),))
            b.op("vector", "tensor_tensor", sb["expw"][:, 0:64], sb["expw"][:, 0:64],
                 sb["mask_w0"][:, :], ALU.mult)
            for cseg in (1, 2, 3):
                o = EXPW_O[cseg]
                b.op("vector", "tensor_tensor", sb["expw"][:, o:o + 192], sb["expw"][:, o:o + 192],
                     sb["mask_wm"][:, :], ALU.mult)
            b.op("vector", "tensor_tensor", sb["expw"][:, 640:768], sb["expw"][:, 640:768],
                 sb["mask_wm"][:, 0:128], ALU.mult)
            b.op("vector", "tensor_tensor", sb["expl"][0:120, :], sb["expl"][0:120, :],
                 sb["mask_lvl"][:, :], ALU.mult, inc="V")
            vM = b.n("V")
            if s == DBG["slab"] and h in (0, 2):
                _tap(b, io, "expw_h%d" % h, sb["expw"][:, 0:768], 128, 768, (("V", vM),))
                _tap(b, io, "expl_h%d" % h, sb["expl"][0:120, :], 120, 512, (("V", vM),))
            b.wait("tensor", (("V", vM),))
            b.op("tensor", "matmul", bank(5, 0, 512)[0:65, :],
                 sb["pvt"][0:120, 65 * h: 65 * h + 65],
                 sb["expl"][0:120, :], start=True, stop=False)
            for cseg in range(5):
                qoff, N = SEG_Q[cseg]
                b.op("tensor", "matmul", bank(5, qoff, N)[0:64, :],
                     sb["vtr"][:, cseg * 1024 + h * 64: cseg * 1024 + h * 64 + 64],
                     sb["expw"][:, EXPW_O[cseg]:EXPW_O[cseg] + N],
                     start=False, stop=False)
                b.op("tensor", "matmul", bank(5, qoff, N)[64:65, :],
                     sb["ones_b"][:, 0:1],
                     sb["expw"][:, EXPW_O[cseg]:EXPW_O[cseg] + N],
                     start=False, stop=(cseg == 4), inc="T" if cseg == 4 else None)
            tU = b.n("T")
            b.wait("vector", (("T", tU),))
            b.op("vector", "reciprocal", sb["row_r"][:, :], bank(5, 0, 512)[64:65, :], inc="V")
            vR = b.n("V")
            b.wait("scalar", (("T", tU),))
            b.op("scalar", "copy", sb["usb"][:, :], bank(5, 0, 512)[0:64, :], inc="S")
            sU = b.n("S")
            b.wait("tensor", (("V", vR),))
            b.op("tensor", "matmul", bank(6, 0, 512)[0:64, :], sb["ones_f"][:, 0:64],
                 sb["row_r"][:, :], start=True, stop=True, inc="T")
            tR = b.n("T")
            b.wait("vector", (("T", tR), ("S", sU)))
            b.op("vector", "tensor_tensor", sb["ao"][hp64:hp64 + 64, c_ * 512:(c_ + 1) * 512],
                 sb["usb"][:, :], bank(6, 0, 512)[0:64, :], ALU.mult, inc="V")
            vAO = b.n("V")
            if s == DBG["slab"] and h in (0, 2):
                _tap(b, io, "usb_h%d" % h, sb["usb"][:, :], 64, 512, (("S", sU),))
                _tap(b, io, "esum_h%d" % h, sb["row_r"][:, :], 1, 512, (("V", vR),))
            b.wait("tensor", (("S", sE), ("V", vAO)))
        vAOall = b.n("V")

        b.wait("tensor", (("V", vAOall),))
        h_ev = []
        for mc in range(CH):
            bk = mc % 2
            if mc >= 2:
                b.wait("tensor", (("V", h_ev[mc - 2]),))
            for kc in range(CH):
                b.op("tensor", "matmul", bank(bk),
                     sb["wp"][:, kc * 1024 + mc * 128: kc * 1024 + mc * 128 + 128],
                     sb["ao"][:, kc * 512:(kc + 1) * 512],
                     start=(kc == 0), stop=(kc == CH - 1), inc="T" if kc == CH - 1 else None)
            tH = b.n("T")
            b.wait("vector", (("T", tH),))
            b.op("vector", "scalar_tensor_tensor", sb["x1"][:, mc * 512:(mc + 1) * 512],
                 bank(bk), sb["bp_c"][:, mc:mc + 1], sb["x"][:, mc * 512:(mc + 1) * 512],
                 ALU.add, ALU.add, inc="V")
            h_ev.append(b.n("V"))
        vHall = b.n("V")
        if s == DBG["slab"]:
            _tap(b, io, "x1_c0", sb["x1"][:, 0:512], 128, 512, (("V", vHall),))
        if s in (0, 1):
            _tap(b, io, "ao_all_s%d" % s, sb["ao"][:, :], 128, CH * 512, (("V", vAOall),))
        b.wait("sync", (("V", vHall),))
        for c in range(CH):
            b.op("sync", "dma_start", out=io["x1"][c * 128:(c + 1) * 128, qs:qs + 512],
                 in_=sb["x1"][:, c * 512:(c + 1) * 512], inc="Y")
        yH = b.n("Y")
        b.wait("tensor", (("Y", yH),))
        b.wait("vector", (("Y", yH),))
        b.wait("scalar", (("Y", yH),))

    return b.n("Y")


def alloc_attn_a_sbuf(nc, es, cm=None):
    ec = es.enter_context
    sb = {}
    if cm is not None:
        for k in ("ones_b", "ones_f", "inv1024", "eps_col", "row_a", "row_b",
                  "row_mu", "row_r", "row_s", "tmpf"):
            sb[k] = cm[k]
    sb["wa"] = ec(nc.sbuf_tensor("s_wa", [128, CH * 3072], BF16))
    sb["wp"] = ec(nc.sbuf_tensor("s_wp", [128, CH * 1024], BF16))
    sb["ba_qk"] = ec(nc.sbuf_tensor("s_ba_qk", [128, 16], F32))
    sb["bq_s"] = ec(nc.sbuf_tensor("bq_s", [128, 8], F32))
    sb["ba_v"] = ec(nc.sbuf_tensor("s_ba_v", [1, 1024], F32))
    sb["ba_vb"] = ec(nc.sbuf_tensor("ba_vb", [1, 1024], BF16))
    sb["bp_c"] = ec(nc.sbuf_tensor("s_bp_c", [128, 8], F32))
    sb["mask_w0"] = ec(nc.sbuf_tensor("s_mask_w0", [128, 64], BF16))
    sb["mask_wm"] = ec(nc.sbuf_tensor("s_mask_wm", [128, 192], BF16))
    sb["mask_lvl"] = ec(nc.sbuf_tensor("s_mask_lvl", [120, 512], BF16))
    sb["svec_row"] = ec(nc.sbuf_tensor("s_svec_row", [1, 120], BF16))
    sb["svec_row0"] = ec(nc.sbuf_tensor("s_svec_row0", [1, 120], BF16))
    sb["ba_k_rowb"] = ec(nc.sbuf_tensor("s_ba_k_rowb", [1, 1024], BF16))
    sb["svec_col"] = ec(nc.sbuf_tensor("s_svec_col", [120, 1], BF16))
    sb["svec_col0"] = ec(nc.sbuf_tensor("s_svec_col0", [120, 1], BF16))
    if "ones_b" not in sb:
        sb["ones_b"] = ec(nc.sbuf_tensor("ones_b", [128, 128], BF16))
    if "ones_f" not in sb:
        sb["ones_f"] = ec(nc.sbuf_tensor("ones_f", [1, 128], F32))
    if "inv1024" not in sb:
        sb["inv1024"] = ec(nc.sbuf_tensor("inv1024", [128, 1], BF16))
    sb["x"] = ec(nc.sbuf_tensor("x_slab", [128, CH * 512], F32))
    sb["x1"] = ec(nc.sbuf_tensor("x1_slab", [128, CH * 512], F32))
    sb["xb"] = ec(nc.sbuf_tensor("xb", [128, CH * 512], BF16))
    sb["xsq"] = ec(nc.sbuf_tensor("xsq", [128, CH * 512], BF16))
    sb["ln1"] = ec(nc.sbuf_tensor("ln1", [128, CH * 512], BF16))
    if "tmpf" not in sb:
        sb["tmpf"] = ec(nc.sbuf_tensor("tmpf", [128, 512], F32))
    sb["qf"] = ec(nc.sbuf_tensor("qf", [128, CH * 512], BF16))
    sb["kroll"] = ec(nc.sbuf_tensor("kroll", [128, CH * 640], BF16))
    sb["vtr"] = ec(nc.sbuf_tensor("vtr", [128, 5 * 1024], BF16))
    sb["xps"] = ec(nc.sbuf_tensor("xps", [128, CH * 120], F32))
    sb["xptail"] = ec(nc.sbuf_tensor("xptail", [128, CH * 32], F32))
    sb["xpg"] = ec(nc.sbuf_tensor("xpg", [128, CH * 120], BF16))
    sb["pkg"] = ec(nc.sbuf_tensor("pkg", [128, CH * 120], BF16))
    sb["pvt"] = ec(nc.sbuf_tensor("pvt", [128, 1040], BF16))
    sb["expw"] = ec(nc.sbuf_tensor("expw", [128, 768], BF16))
    sb["expl"] = ec(nc.sbuf_tensor("expl", [120, 512], BF16))
    sb["usb"] = ec(nc.sbuf_tensor("usb", [64, 512], BF16))
    sb["ao"] = ec(nc.sbuf_tensor("ao", [128, CH * 512], BF16))
    if "eps_col" not in sb:
        sb["eps_col"] = ec(nc.sbuf_tensor("eps_col", [128, 1], F32))
    if "row_a" not in sb:
        sb["row_a"] = ec(nc.sbuf_tensor("row_a", [1, 512], F32))
    if "row_b" not in sb:
        sb["row_b"] = ec(nc.sbuf_tensor("row_b", [1, 512], F32))
    if "row_mu" not in sb:
        sb["row_mu"] = ec(nc.sbuf_tensor("row_mu", [1, 512], F32))
    if "row_r" not in sb:
        sb["row_r"] = ec(nc.sbuf_tensor("row_r", [1, 512], F32))
    if "row_s" not in sb:
        sb["row_s"] = ec(nc.sbuf_tensor("row_s", [1, 512], F32))
    return sb


def build_attn_a(x1_kind="ExternalOutput", debug=False):
    nc = bass.Bass("TRN2", target_bir_lowering=False)
    io = {}
    io["xin"] = nc.dram_tensor("xin", [D, T], F32, kind="ExternalInput").ap()
    io["wa"] = nc.dram_tensor("wa", [D, 3072], F32, kind="ExternalInput").ap()
    io["wp"] = nc.dram_tensor("wp", [D, D], F32, kind="ExternalInput").ap()
    io["ba_qk"] = nc.dram_tensor("ba_qk", [128, 16], F32, kind="ExternalInput").ap()
    io["ba_v"] = nc.dram_tensor("ba_v", [1, 1024], F32, kind="ExternalInput").ap()
    io["bp_c"] = nc.dram_tensor("bp_c", [128, 8], F32, kind="ExternalInput").ap()
    io["ba_k_row"] = nc.dram_tensor("ba_k_row", [1, 1024], F32, kind="ExternalInput").ap()
    for k in ("a_mask_w0", "a_mask_wm", "a_mask_lvl", "a_svec_row", "a_svec_col",
              "a_svec_row0", "a_svec_col0", "ident"):
        shp = {"a_mask_w0": [128, 64], "a_mask_wm": [128, 192], "a_mask_lvl": [120, 512],
               "a_svec_row": [1, 120], "a_svec_col": [120, 1], "a_svec_row0": [1, 120],
               "a_svec_col0": [120, 1], "ident": [128, 128]}[k]
        io[k] = nc.dram_tensor(k, shp, F32, kind="ExternalInput").ap()
    io["x1"] = nc.dram_tensor("x1", [D, T], F32, kind=x1_kind).ap()
    if debug:
        DBG["taps"] = []
        DBG["off"] = 0
        io["dbg"] = nc.dram_tensor("dbg", [128, 65536], F32, kind="ExternalOutput").ap()

    with contextlib.ExitStack() as es:
        sb = alloc_attn_a_sbuf(nc, es)
        ps = es.enter_context(nc.psum_tensor("PS", [128, 4096], F32))
        b = Bld(nc, es)
        for s_ in ("T", "V", "S", "G", "Y"):
            b.sem(s_)
        blk = es.enter_context(nc.Block())
        emit_attn_a_with_post(b, io, sb, ps)
        for eng in ENGINES:
            ops = b.q[eng]
            if not ops:
                continue
            getattr(blk, eng)(lambda e, ops=ops: [f(e) for f in ops])
    return nc


def emit_attn_a_with_post(b, io, sb, ps):
    b.op("gpsimd", "dma_start", out=sb["ba_vb"][:, :], in_=io["ba_v"][:, :], inc="G")
    yfin = emit_attn_a(b, io, sb, ps)
    b.wait("sync", (("Y", yfin),))




def alloc_common_sbuf(nc, es):
    ec = es.enter_context
    cm = {}
    cm["ones_b"] = ec(nc.sbuf_tensor("c_ones_b", [128, 128], BF16))
    cm["ones_f"] = ec(nc.sbuf_tensor("c_ones_f", [1, 128], F32))
    cm["inv1024"] = ec(nc.sbuf_tensor("c_inv1024", [128, 1], BF16))
    cm["eps_col"] = ec(nc.sbuf_tensor("c_eps_col", [128, 1], F32))
    cm["identf"] = ec(nc.sbuf_tensor("c_identf", [128, 128], F32))
    cm["row_a"] = ec(nc.sbuf_tensor("c_row_a", [1, 512], F32))
    cm["row_b"] = ec(nc.sbuf_tensor("c_row_b", [1, 512], F32))
    cm["row_mu"] = ec(nc.sbuf_tensor("c_row_mu", [1, 512], F32))
    cm["row_r"] = ec(nc.sbuf_tensor("c_row_r", [1, 512], F32))
    cm["row_s"] = ec(nc.sbuf_tensor("c_row_s", [1, 512], F32))
    cm["negrow"] = ec(nc.sbuf_tensor("c_negrow", [1, 512], BF16))
    cm["rows8"] = ec(nc.sbuf_tensor("c_rows8", [8, 512], F32))
    cm["tmpf"] = ec(nc.sbuf_tensor("c_tmpf", [128, 512], F32))
    cm["tmpf2"] = ec(nc.sbuf_tensor("c_tmpf2", [128, 512], F32))
    return cm


def emit_common_init(b, io, cm):
    b.op("vector", "memset", cm["ones_b"][:, :], 1.0)
    b.op("vector", "memset", cm["ones_f"][:, :], 1.0)
    b.op("vector", "memset", cm["inv1024"][:, :], 1.0 / 1024.0)
    b.op("vector", "memset", cm["eps_col"][:, :], 1e-5)
    b.op("gpsimd", "dma_start", out=cm["identf"][:, :], in_=io["ident"][:, :], inc="G")


def emit_ln_stats(b, cm, ps, bf_src_fn, f32_src_fn, dst_fn, eps_bias, nch=CH, prep_fn=None):
    PS = ps
    mmt = []
    for c in range(nch):
        if c >= 2:
            b.wait("scalar", (("T", mmt[c - 2]),))
        if prep_fn is not None:
            prep_fn(c)
        b.op("scalar", "activation", cm["sqb"][:, (c % 2) * 512:(c % 2) * 512 + 512],
             bf_src_fn(c), AF.Square, inc="S")
        sc_ = b.n("S")
        b.wait("tensor", (("S", sc_),))
        b.op("tensor", "matmul", PS[0:1, 4 * 512:5 * 512], cm["inv1024"][:, :],
             bf_src_fn(c), start=(c == 0), stop=(c == nch - 1))
        b.op("tensor", "matmul", PS[0:1, 5 * 512:6 * 512], cm["inv1024"][:, :],
             cm["sqb"][:, (c % 2) * 512:(c % 2) * 512 + 512],
             start=(c == 0), stop=(c == nch - 1), inc="T")
        mmt.append(b.n("T"))
    tSt = b.n("T")
    b.wait("vector", (("T", tSt),))
    b.op("vector", "tensor_copy", cm["row_mu"][:, :], PS[0:1, 4 * 512:5 * 512])
    b.op("vector", "tensor_tensor", cm["row_a"][:, :], cm["row_mu"][:, :], cm["row_mu"][:, :], ALU.mult)
    b.op("vector", "tensor_tensor", cm["row_b"][:, :], PS[0:1, 5 * 512:6 * 512], cm["row_a"][:, :],
         ALU.subtract)
    vv = b.n("V")
    b.wait("scalar", (("V", vv),))
    b.op("scalar", "activation", cm["row_a"][:, :], cm["row_b"][:, :], AF.Sqrt,
         bias=eps_bias, scale=1.0)
    sS2 = b.n("S")
    b.wait("vector", (("S", sS2),))
    b.op("vector", "reciprocal", cm["row_r"][:, :], cm["row_a"][:, :])
    b.op("vector", "scalar_tensor_tensor", cm["row_s"][:, :], cm["row_mu"][:, :], -1.0,
         cm["row_r"][:, :], ALU.mult, ALU.mult)
    vv2 = b.n("V")
    b.wait("tensor", (("V", vv2),))
    b.op("tensor", "matmul", PS[:, 6 * 512:7 * 512], cm["ones_f"][:, :], cm["row_r"][:, :],
         start=True, stop=True)
    b.op("tensor", "matmul", PS[:, 7 * 512:8 * 512], cm["ones_f"][:, :], cm["row_s"][:, :],
         start=True, stop=True, inc="T")
    tB2 = b.n("T")
    b.wait("vector", (("T", tB2),))
    if "rbsnap" in cm:
        b.op("vector", "tensor_copy", cm["rbsnap"][:, :], PS[:, 6 * 512:7 * 512])
        b.op("vector", "tensor_copy", cm["sbsnap"][:, :], PS[:, 7 * 512:8 * 512])
    for c in range(nch):
        dst_fn(c, PS[:, 6 * 512:7 * 512], PS[:, 7 * 512:8 * 512])


def emit_ffn_phaseA(b, io, sb, cm, ps, l, src_name, dst_name, apply_ln):
    nc = b.nc
    PS = ps

    def bank(i, lo=0, n=512):
        return PS[:, 512 * i + lo: 512 * i + lo + n]

    src = io[src_name]
    dst = io[dst_name]
    for kc in range(CH):
        b.op("gpsimd", "dma_start", out=sb["wqkv"][:, kc * 1536:(kc + 1) * 1536],
             in_=io["f_wqkv"][l, kc * 128:(kc + 1) * 128, :], inc="G")
    for kc in range(4):
        b.op("gpsimd", "dma_start", out=sb["wo"][:, kc * 1024:(kc + 1) * 1024],
             in_=io["f_wo"][l, kc * 128:(kc + 1) * 128, :], inc="G")
    b.op("gpsimd", "dma_start", out=sb["rotm"][:, :], in_=io["f_rot"][:, :], inc="G")
    b.op("gpsimd", "dma_start", out=sb["self_"][:, :], in_=io["f_sel"][:, :], inc="G")
    b.op("gpsimd", "dma_start", out=sb["bself"][:, :], in_=io["f_bsel"][:, :], inc="G")
    b.op("gpsimd", "dma_start", out=sb["maskw"][:, :], in_=io["f_maskw"][:, :], inc="G")
    b.op("gpsimd", "dma_start", out=sb["qscol"][:, :], in_=io["f_qs"][l:l + 1, :], inc="G")
    b.op("gpsimd", "dma_start", out=sb["kscol"][:, :], in_=io["f_ks"][l:l + 1, :], inc="G")
    g0 = b.n("G")
    b.op("vector", "memset", sb["vtr2"][:, :], 0.0)
    b.op("vector", "memset", bass.AP(sb["vtr2"], 64, [[4160, 128], [65, 64], [1, 1]]), 1.0)
    for c in range(4):
        b.op("vector", "memset", sb["kroll2"][:, c * 1024: c * 1024 + 512], 0.0)
    for c in range(CH):
        b.op("vector", "memset", sb["zin"][:, c * 513: c * 513 + 1], 0.0)
        b.op("vector", "memset", sb["zinb"][:, c * 513: c * 513 + 1], 0.0)
    v0 = b.n("V")

    def hview(buf, c, w=512):
        off = 1 if c < 4 else 0
        return buf[:, c * 513 + off: c * 513 + off + w]

    def cview(buf, c, w=512):
        return buf[:, c * 513 + 1: c * 513 + 1 + w]

    prev_vprep = [0]
    for s in range(NSLAB):
        qs = s * SLAB
        if s > 0:
            b.op("vector", "tensor_copy",
                 sb["zin"][:, :].rearrange("p (c w) -> p c w", w=513)[:, :, 0:1],
                 sb["zin"][:, :].rearrange("p (c w) -> p c w", w=513)[:, :, 512:513])
            b.op("vector", "tensor_copy",
                 sb["zinb"][:, :].rearrange("p (c w) -> p c w", w=513)[:, :, 0:1],
                 sb["zinb"][:, :].rearrange("p (c w) -> p c w", w=513)[:, :, 512:513])
            for c in range(4):
                b.op("vector", "tensor_copy", sb["kroll2"][:, c * 1024: c * 1024 + 512],
                     sb["kroll2"][:, c * 1024 + 512: c * 1024 + 1024])
            b.op("vector", "tensor_copy", sb["vtr2"][:, 0:2080], sb["vtr2"][:, 2080:4160])
        vRoll = b.n("V")
        if io.get("dbg") is not None and s == DBG["slab"] and s > 0:
            _tap(b, io, "fza%d_krollpost" % l, sb["kroll2"][:, 0:512], 128, 512, (("V", vRoll),))

        if apply_ln:
            for c in range(CH):
                b.op("sync", "dma_start", out=sb["xr"][:, c * 512:(c + 1) * 512],
                     in_=src[c * 128:(c + 1) * 128, qs:qs + 512], inc="Y")
        else:
            for c in range(CH):
                b.op("sync", "dma_start", out=cview(sb["zin"], c),
                     in_=src[c * 128:(c + 1) * 128, qs:qs + 512],
                     w=(("V", vRoll),), inc="Y")
        for t_ in range(4):
            nm = ("f_AQ", "f_BQ", "f_AK", "f_BK")[t_]
            b.op("gpsimd", "dma_start", out=sb["tabs"][:, t_ * 512:(t_ + 1) * 512],
                 in_=io[nm][:, qs:qs + 512],
                 w=(("V", prev_vprep[0]),) if s > 0 else (), inc="G")
        yL = b.n("Y")
        gT = b.n("G")

        if apply_ln:
            b.wait("scalar", (("Y", yL), ("V", vRoll)))

            def bf_src(c):
                b2_ = None
                return sb["xrb"][:, c * 512:(c + 1) * 512]
            for c in range(CH):
                b.op("scalar", "copy", sb["xrb"][:, c * 512:(c + 1) * 512],
                     sb["xr"][:, c * 512:(c + 1) * 512], inc="S" if c == CH - 1 else None)

            def dstf(c, rb, sbp):
                b.op("vector", "tensor_tensor", cm["tmpf"][:, :], sb["xr"][:, c * 512:(c + 1) * 512],
                     rb, ALU.mult)
                b.op("vector", "tensor_tensor", cview(sb["zin"], c), cm["tmpf"][:, :], sbp, ALU.add)
                b.op("vector", "tensor_copy", cview(sb["zinb"], c), cview(sb["zin"], c))
            emit_ln_stats(b, cm, PS, bf_src, None, dstf, cm["eps_col"][0:1, :])
        else:
            b.wait("scalar", (("Y", yL),))
            for c in range(CH):
                b.op("scalar", "copy", cview(sb["zinb"], c), cview(sb["zin"], c),
                     inc="S" if c == CH - 1 else None)
        vZ = b.n("V")
        sZ = b.n("S")
        if 1 in DBG.get("joins", ()):
            _phase_join(b)
        if io.get("dbg") is not None and s == DBG["slab"]:
            _tap(b, io, "fza%d_zin_c0" % l, sb["zin"][:, 0:513], 128, 513, (("V", vZ), ("S", sZ)))
            if apply_ln:
                _tap(b, io, "fza%d_rowr" % l, cm["row_r"][:, :], 1, 512, (("V", vZ),))
                _tap(b, io, "fza%d_rows" % l, cm["row_s"][:, :], 1, 512, (("V", vZ),))
                _tap(b, io, "fza%d_xrb_c0" % l, sb["xrb"][:, 0:512], 128, 512, (("V", vZ),))
                if "rbsnap" in cm:
                    _tap(b, io, "fza%d_rbsnap" % l, cm["rbsnap"][:, :], 128, 512, (("V", vZ),))
                    _tap(b, io, "fza%d_sbsnap" % l, cm["sbsnap"][:, :], 128, 512, (("V", vZ),))

        b.wait("tensor", (("V", vZ), ("S", sZ), ("G", g0), ("G", gT), ("Y", yL)))
        qkv_ev = []
        for mc in range(8):
            bk = mc % 2
            if mc >= 2:
                b.wait("tensor", (("S", qkv_ev[mc - 2]),))
            for kc in range(CH):
                b.op("tensor", "matmul", bank(bk),
                     sb["wqkv"][:, kc * 1536 + mc * 128: kc * 1536 + mc * 128 + 128],
                     hview(sb["zinb"], kc),
                     start=(kc == 0), stop=(kc == CH - 1), inc="T" if kc == CH - 1 else None)
            tE = b.n("T")
            b.wait("scalar", (("T", tE),))
            if mc < 4:
                b.op("scalar", "copy", sb["qraw"][:, mc * 512:(mc + 1) * 512], bank(bk), inc="S")
            else:
                b.op("scalar", "copy", sb["kraw"][:, (mc - 4) * 512:(mc - 3) * 512], bank(bk), inc="S")
            qkv_ev.append(b.n("S"))
        b.wait("tensor", (("S", b.n("S")),))
        for i in range(4):
            bk = i % 2
            for kc in range(CH):
                b.op("tensor", "matmul", bank(bk),
                     hview(sb["zinb"], kc)[:, i * 128: i * 128 + 128],
                     sb["wqkv"][:, kc * 1536 + 1024: kc * 1536 + 1536],
                     start=(kc == 0), stop=(kc == CH - 1), inc="T" if kc == CH - 1 else None)
            tv = b.n("T")
            b.wait("scalar", (("T", tv),))
            b.op("scalar", "copy",
                 sb["vtr2"][:, (4 + i) * 520: (4 + i) * 520 + 520].rearrange(
                     "p (h d) -> p h d", d=65)[:, :, 0:64],
                 bank(bk).rearrange("p (h d) -> p h d", d=64), inc="S")
            sv_ = b.n("S")
            b.wait("tensor", (("S", sv_),))

        if 2 in DBG.get("joins", ()):
            _phase_join(b)
        if io.get("dbg") is not None and s == DBG["slab"]:
            _tap(b, io, "fza%d_qraw_c0" % l, sb["qraw"][:, 0:512], 128, 512, (("S", b.n("S")),))
            _tap(b, io, "fza%d_kraw_c0" % l, sb["kraw"][:, 0:512], 128, 512, (("S", b.n("S")),))
            _tap(b, io, "fza%d_vtr2" % l, sb["vtr2"][:, 0:4160], 128, 4160, (("S", b.n("S")),))
        for (raw, qsc, A_o, B_o, outbuf) in (
            (sb["qraw"], sb["qscol"], 0, 512, None),
            (sb["kraw"], sb["kscol"], 1024, 1536, None),
        ):
            is_q = raw is sb["qraw"]
            nrow = 0 if is_q else 32
            selt = []
            for c in range(4):
                if c >= 2:
                    b.wait("scalar", (("T", selt[c - 2]),))
                b.op("scalar", "activation", sb["sq"][:, (c % 2) * 512:(c % 2) * 512 + 512],
                     raw[:, c * 512:(c + 1) * 512], AF.Square, inc="S")
                sS = b.n("S")
                b.wait("tensor", (("S", sS),))
                b.op("tensor", "matmul", PS[nrow:nrow + 8, 7 * 512:8 * 512],
                     sb["self_"][:, c * 8:(c + 1) * 8], sb["sq"][:, (c % 2) * 512:(c % 2) * 512 + 512],
                     start=(c == 0), stop=(c == 3), inc="T")
                selt.append(b.n("T"))
            tN = b.n("T")
            b.wait("scalar", (("T", tN),))
            b.op("scalar", "activation", cm["rows8"][:, :], PS[nrow:nrow + 8, 7 * 512:8 * 512],
                 AF.Sqrt, bias=0.0, scale=1.0, inc="S")
            sN = b.n("S")
            b.wait("vector", (("S", sN),))
            b.op("vector", "tensor_scalar_max", cm["rows8"][:, :], cm["rows8"][:, :], 1e-12)
            b.op("vector", "reciprocal", cm["rows8"][:, :], cm["rows8"][:, :], inc="V")
            vN = b.n("V")
            last_rot = [None]
            for c in range(4):
                if last_rot[0] is not None:
                    b.op("vector", "tensor_scalar_mul", sb["qsr"][:, :], raw[:, c * 512:(c + 1) * 512],
                         qsc[:, 0:1], w=(("T", last_rot[0]),), inc="V")
                else:
                    b.op("vector", "tensor_scalar_mul", sb["qsr"][:, :], raw[:, c * 512:(c + 1) * 512],
                         qsc[:, 0:1], inc="V")
                vq = b.n("V")
                b.wait("tensor", (("V", vq), ("V", vN)))
                b.op("tensor", "matmul", bank(2), sb["rotm"][:, :], sb["qsr"][:, :],
                     start=True, stop=True)
                b.op("tensor", "matmul", bank(3), sb["bself"][:, c * 128:(c + 1) * 128],
                     cm["rows8"][:, :], start=True, stop=True, inc="T")
                tr_ = b.n("T")
                last_rot[0] = tr_
                b.op("vector", "tensor_tensor", cm["tmpf"][:, :], sb["qsr"][:, :],
                     sb["tabs"][:, A_o:A_o + 512], ALU.mult)
                b.wait("vector", (("T", tr_),))
                b.op("vector", "tensor_tensor", cm["tmpf2"][:, :], bank(2),
                     sb["tabs"][:, B_o:B_o + 512], ALU.mult)
                b.op("vector", "tensor_tensor", cm["tmpf"][:, :], cm["tmpf"][:, :],
                     cm["tmpf2"][:, :], ALU.add)
                dsts = (sb["qprime"][:, c * 512:(c + 1) * 512] if is_q
                        else sb["kroll2"][:, c * 1024 + 512: c * 1024 + 1024])
                b.op("vector", "tensor_tensor", dsts, cm["tmpf"][:, :], bank(3), ALU.mult,
                     inc="V")
                vPr = b.n("V")
                if io.get("dbg") is not None and s == DBG["slab"] and (not is_q) and c == 0:
                    _tap(b, io, "fzp%d_qsr" % l, sb["qsr"][:, :], 128, 512, (("V", vPr),))
                    _tap(b, io, "fzp%d_rows8" % l, cm["rows8"][:, :], 8, 512, (("V", vPr),))
                    b.op("vector", "tensor_copy", cm["tmpf2"][:, :], bank(2), inc="V")
                    _tap(b, io, "fzp%d_rot" % l, cm["tmpf2"][:, :], 128, 512, (("V", b.n("V")),))
                    b.op("vector", "tensor_copy", cm["tmpf2"][:, :], bank(3), inc="V")
                    _tap(b, io, "fzp%d_rb" % l, cm["tmpf2"][:, :], 128, 512, (("V", b.n("V")),))
                b.wait("tensor", (("V", vPr),))
        vPrep = b.n("V")

        if io.get("dbg") is not None and s in (DBG["slab"], DBG["slab"] - 1):
            _tap(b, io, "fza%d_qprime_c0_s%d" % (l, s), sb["qprime"][:, 0:512], 128, 512, (("V", vPrep),))
            _tap(b, io, "fza%d_kroll2_c0_s%d" % (l, s), sb["kroll2"][:, 0:1024], 128, 1024, (("V", vPrep),))
        b.wait("tensor", (("V", vPrep),))
        for h in range(8):
            hp, ch = h % 2, h // 2
            hp64 = hp * 64
            for a in range(4):
                if s == 0:
                    wspan = 128 * (a + 1)
                    b.op("tensor", "matmul", bank(2, 0, wspan),
                         sb["qprime"][hp64:hp64 + 64, ch * 512 + a * 128: ch * 512 + a * 128 + 128],
                         sb["kroll2"][hp64:hp64 + 64, ch * 1024 + 512: ch * 1024 + 512 + wspan],
                         start=True, stop=True, inc="T")
                else:
                    b.op("tensor", "matmul", bank(2, 0, 512),
                         sb["qprime"][hp64:hp64 + 64, ch * 512 + a * 128: ch * 512 + a * 128 + 128],
                         sb["kroll2"][hp64:hp64 + 64, ch * 1024 + a * 128: ch * 1024 + a * 128 + 512],
                         start=True, stop=True)
                    b.op("tensor", "matmul", bank(3, 0, 128),
                         sb["qprime"][hp64:hp64 + 64, ch * 512 + a * 128: ch * 512 + a * 128 + 128],
                         sb["kroll2"][hp64:hp64 + 64, ch * 1024 + a * 128 + 512: ch * 1024 + a * 128 + 640],
                         start=True, stop=True, inc="T")
                tSm = b.n("T")
                b.wait("vector", (("T", tSm),))
                if s == 0:
                    b.op("vector", "reduce_max", sb["negm4"][:, a:a + 1], bank(2, 0, 128 * (a + 1)),
                         axis=AX.X, negate=True, inc="V")
                else:
                    b.op("vector", "reduce_max", sb["negm4"][:, a:a + 1], PS[:, 2 * 512: 2 * 512 + 640],
                         axis=AX.X, negate=True, inc="V")
                vm_ = b.n("V")
                b.wait("tensor", (("V", vm_),))
            for a in range(4):
                b.op("tensor", "transpose", PS[0:1, 7 * 512 + 128 * a: 7 * 512 + 128 * (a + 1)],
                     sb["negm4"][:, a:a + 1], cm["identf"][:, :],
                     inc="T" if a == 3 else None)
            tTr = b.n("T")
            b.wait("vector", (("T", tTr),))
            b.op("vector", "tensor_copy", cm["negrow"][:, :], PS[0:1, 7 * 512: 8 * 512], inc="V")
            vNg = b.n("V")
            b.wait("tensor", (("V", vNg),))
            jlo = 4 if s == 0 else 0
            for j in range(jlo, 8):
                pb = 4 + (j % 2)
                b.op("tensor", "matmul", bank(pb),
                     sb["kroll2"][hp64:hp64 + 64, ch * 1024 + 128 * j: ch * 1024 + 128 * j + 128],
                     sb["qprime"][hp64:hp64 + 64, ch * 512: ch * 512 + 512],
                     start=True, stop=False)
                b.op("tensor", "matmul", bank(pb), cm["ones_b"][0:1, 0:128],
                     cm["negrow"][:, :], start=False, stop=True, inc="T")
                tkm = b.n("T")
                b.wait("scalar", (("T", tkm),))
                b.op("scalar", "activation", sb["expf"][:, (j % 2) * 512: (j % 2) * 512 + 512],
                     bank(pb), AF.Exp, inc="S")
                se_ = b.n("S")
                b.wait("vector", (("S", se_),))
                b.op("vector", "tensor_tensor", sb["expf"][:, (j % 2) * 512: (j % 2) * 512 + 512],
                     sb["expf"][:, (j % 2) * 512: (j % 2) * 512 + 512],
                     sb["maskw"][:, 896 - 128 * j: 1408 - 128 * j], ALU.mult, inc="V")
                vme = b.n("V")
                b.wait("tensor", (("V", vme),))
                b.op("tensor", "matmul", bank(6, 0, 512)[0:65, :],
                     sb["vtr2"][:, 520 * j + 65 * h: 520 * j + 65 * h + 65],
                     sb["expf"][:, (j % 2) * 512: (j % 2) * 512 + 512],
                     start=(j == jlo), stop=(j == 7), inc="T" if j == 7 else None)
            tU = b.n("T")
            b.wait("vector", (("T", tU),))
            b.op("vector", "reciprocal", cm["row_r"][:, :], bank(6, 0, 512)[64:65, :], inc="V")
            vR = b.n("V")
            b.wait("scalar", (("T", tU),))
            b.op("scalar", "copy", sb["usb"][:, :], bank(6, 0, 512)[0:64, :], inc="S")
            sU = b.n("S")
            b.wait("tensor", (("V", vR),))
            b.op("tensor", "matmul", bank(7, 0, 512)[0:64, :], cm["ones_f"][:, 0:64],
                 cm["row_r"][:, :], start=True, stop=True, inc="T")
            tR = b.n("T")
            b.wait("vector", (("T", tR), ("S", sU)))
            b.op("vector", "tensor_tensor", sb["ao2"][hp64:hp64 + 64, ch * 512:(ch + 1) * 512],
                 sb["usb"][:, :], bank(7, 0, 512)[0:64, :], ALU.mult, inc="V")
            vAO = b.n("V")
            if io.get("dbg") is not None and s == DBG["slab"] and h == 0:
                _tap(b, io, "fza%d_negrow_h0" % l, cm["negrow"][:, :], 1, 512, (("V", vNg),))
                _tap(b, io, "fza%d_expf7_h0" % l, sb["expf"][:, 512:1024], 128, 512, (("V", b.n("V")),))
                _tap(b, io, "fza%d_usb_h0" % l, sb["usb"][:, :], 64, 512, (("S", sU),))
                _tap(b, io, "fza%d_rr_h0" % l, cm["row_r"][:, :], 1, 512, (("V", vR),))
            b.wait("tensor", (("S", sU), ("V", vAO)))
        vAOall = b.n("V")
        _phase_join(b)

        b.wait("tensor", (("V", vAOall),))
        wo_ev = []
        for mc in range(CH):
            bk = mc % 2
            if mc >= 2:
                b.wait("tensor", (("V", wo_ev[mc - 2]),))
            for kc in range(4):
                b.op("tensor", "matmul", bank(bk),
                     sb["wo"][:, kc * 1024 + mc * 128: kc * 1024 + mc * 128 + 128],
                     sb["ao2"][:, kc * 512:(kc + 1) * 512],
                     start=(kc == 0), stop=(kc == 3), inc="T" if kc == 3 else None)
            tW = b.n("T")
            b.wait("vector", (("T", tW),))
            b.op("vector", "tensor_tensor", sb["zmidA"][:, mc * 512:(mc + 1) * 512],
                 bank(bk), cview(sb["zin"], mc), ALU.add, inc="V")
            wo_ev.append(b.n("V"))
        vWall = b.n("V")
        b.wait("sync", (("V", vWall),))
        for c in range(CH):
            b.op("sync", "dma_start", out=dst[c * 128:(c + 1) * 128, qs:qs + 512],
                 in_=sb["zmidA"][:, c * 512:(c + 1) * 512], inc="Y")
        if io.get("dbg") is not None and s == DBG["slab"]:
            _tap(b, io, "fza%d_ao2" % l, sb["ao2"][:, 0:2048], 128, 2048, (("V", vAOall),))
            _tap(b, io, "fza%d_zmid_c0" % l, sb["zmidA"][:, 0:512], 128, 512, (("V", vWall),))
        yS = b.n("Y")
        b.wait("tensor", (("Y", yS),))
        b.wait("vector", (("Y", yS),))
        b.wait("scalar", (("Y", yS),))
    return b.n("Y")


def emit_ffn_phaseB(b, io, sb, cm, ps, l, src_name, dst_name, x_res_name):
    nc = b.nc
    PS = ps

    def bank(i, lo=0, n=512):
        return PS[:, 512 * i + lo: 512 * i + lo + n]

    src = io[src_name]
    dst = io[dst_name]
    xres = io[x_res_name] if x_res_name else None
    for kc in range(CH):
        b.op("gpsimd", "dma_start", out=sb["w1p"][:, kc * 5632:(kc + 1) * 5632],
             in_=io["f_w1p"][l, kc * 128:(kc + 1) * 128, :], inc="G")
    for kc in range(22):
        b.op("gpsimd", "dma_start", out=sb["w2p"][:, kc * 1024:(kc + 1) * 1024],
             in_=io["f_w2p"][l, kc * 128:(kc + 1) * 128, :], inc="G")
    b.op("gpsimd", "dma_start", out=sb["b1c"][:, :], in_=io["f_b1p"][:, 44 * l: 44 * (l + 1)], inc="G")
    b.op("gpsimd", "dma_start", out=sb["b2c"][:, :], in_=io["f_b2"][:, 8 * l: 8 * (l + 1)], inc="G")
    b.op("gpsimd", "dma_start", out=sb["lgc"][:, :], in_=io["f_lg"][:, 8 * l: 8 * (l + 1)], inc="G")
    b.op("gpsimd", "dma_start", out=sb["lbc"][:, :], in_=io["f_lb"][:, 8 * l: 8 * (l + 1)], inc="G")
    g0 = b.n("G")

    def hview(buf, c, w=512):
        off = 1 if c < 4 else 0
        return buf[:, c * 513 + off: c * 513 + off + w]

    def cview(buf, c, w=512):
        return buf[:, c * 513 + 1: c * 513 + 1 + w]

    for s in range(NSLAB):
        qs = s * SLAB
        if s == 0:
            for c in range(CH):
                b.op("vector", "memset", sb["zm"][:, c * 513: c * 513 + 1], 0.0, inc="V")
            vz0 = b.n("V")
            b.wait("sync", (("V", vz0),))
            for c in range(CH):
                b.op("sync", "dma_start", out=cview(sb["zm"], c),
                     in_=src[c * 128:(c + 1) * 128, qs:qs + 512], inc="Y")
        else:
            for c in range(CH):
                b.op("sync", "dma_start", out=sb["zm"][:, c * 513: c * 513 + 513],
                     in_=src[c * 128:(c + 1) * 128, qs - 1:qs + 512], inc="Y")
        yL = b.n("Y")
        b.wait("scalar", (("Y", yL), ("G", g0)))

        def prep(c):
            b.op("scalar", "copy", sb["zc"][:, (c % 2) * 513:(c % 2) * 513 + 513],
                 sb["zm"][:, c * 513: c * 513 + 513])

        def bf_src(c):
            off = 1 if c < 4 else 0
            return sb["zc"][:, (c % 2) * 513 + off: (c % 2) * 513 + off + 512]

        def dstf(c, rb, sbp):
            b.op("vector", "tensor_tensor", cm["tmpf"][:, :], hview(sb["zm"], c), rb, ALU.mult)
            b.op("vector", "tensor_tensor", cm["tmpf2"][:, :], cm["tmpf"][:, :], sbp, ALU.add,
                 inc="V")
            vx = b.n("V")
            b.wait("scalar", (("V", vx),))
            b.op("scalar", "activation", sb["ln2b"][:, c * 512:(c + 1) * 512], cm["tmpf2"][:, :],
                 AF.Identity, bias=sb["lbc"][:, c:c + 1], scale=sb["lgc"][:, c:c + 1], inc="S")
            sx = b.n("S")
            b.wait("vector", (("S", sx),))
        emit_ln_stats(b, cm, PS, bf_src, None, dstf, cm["eps_col"][0:1, :], prep_fn=prep)
        vLN = b.n("V")
        sLN = b.n("S")

        b.wait("tensor", (("V", vLN), ("S", sLN)))
        w1_ev = []
        for p in range(22):
            bA = 0 if p % 2 == 0 else 4
            bB = 1 if p % 2 == 0 else 5
            if p >= 2:
                b.wait("tensor", (("V", w1_ev[p - 2]),))
            for kc in range(CH):
                b.op("tensor", "matmul", bank(bA),
                     sb["w1p"][:, kc * 5632 + 256 * p: kc * 5632 + 256 * p + 128],
                     sb["ln2b"][:, kc * 512:(kc + 1) * 512],
                     start=(kc == 0), stop=(kc == CH - 1))
            for kc in range(CH):
                b.op("tensor", "matmul", bank(bB),
                     sb["w1p"][:, kc * 5632 + 256 * p + 128: kc * 5632 + 256 * p + 256],
                     sb["ln2b"][:, kc * 512:(kc + 1) * 512],
                     start=(kc == 0), stop=(kc == CH - 1), inc="T" if kc == CH - 1 else None)
            tP = b.n("T")
            b.wait("scalar", (("T", tP),))
            b.op("scalar", "activation", sb["gel"][:, (p % 2) * 512:(p % 2) * 512 + 512],
                 bank(bB), AF.Gelu,
                 bias=sb["b1c"][:, 2 * p + 1: 2 * p + 2], scale=1.0, inc="S")
            sG = b.n("S")
            b.wait("vector", (("S", sG),))
            b.op("vector", "scalar_tensor_tensor", sb["hg"][:, p * 512:(p + 1) * 512],
                 bank(bA), sb["b1c"][:, 2 * p: 2 * p + 1],
                 sb["gel"][:, (p % 2) * 512:(p % 2) * 512 + 512],
                 ALU.add, ALU.mult, inc="V")
            w1_ev.append(b.n("V"))
        vHG = b.n("V")
        b.wait("tensor", (("V", vHG),))

        b.wait("tensor", (("V", vHG),))
        w2_ev = []
        for mc in range(CH):
            bk = 2 + (mc % 2)
            if mc >= 2:
                b.wait("tensor", (("V", w2_ev[mc - 2]),))
            for kc in range(22):
                b.op("tensor", "matmul", bank(bk),
                     sb["w2p"][:, kc * 1024 + mc * 128: kc * 1024 + mc * 128 + 128],
                     sb["hg"][:, kc * 512:(kc + 1) * 512],
                     start=(kc == 0), stop=(kc == 21), inc="T" if kc == 21 else None)
            tW = b.n("T")
            if xres is not None:
                x1c = cm["sqb"][:, :].bitcast(F32)
                b.op("sync", "dma_start", out=x1c,
                     in_=xres[mc * 128:(mc + 1) * 128, qs:qs + 512],
                     w=(("T", tW),), inc="Y")
                yx = b.n("Y")
                b.wait("vector", (("T", tW), ("Y", yx)))
                b.op("vector", "scalar_tensor_tensor", cm["tmpf"][:, :],
                     bank(bk), sb["b2c"][:, mc:mc + 1], cview(sb["zm"], mc), ALU.add, ALU.add)
                b.op("vector", "tensor_tensor", sb["zoc"][:, (mc % 2) * 512:(mc % 2) * 512 + 512],
                     cm["tmpf"][:, :], cm["sqb"][:, :].bitcast(F32), ALU.add, inc="V")
            else:
                b.wait("vector", (("T", tW),))
                b.op("vector", "scalar_tensor_tensor", sb["zoc"][:, (mc % 2) * 512:(mc % 2) * 512 + 512],
                     bank(bk), sb["b2c"][:, mc:mc + 1], cview(sb["zm"], mc), ALU.add, ALU.add,
                     inc="V")
            vO = b.n("V")
            w2_ev.append(vO)
            b.wait("sync", (("V", vO),))
            b.op("sync", "dma_start", out=dst[mc * 128:(mc + 1) * 128, qs:qs + 512],
                 in_=sb["zoc"][:, (mc % 2) * 512:(mc % 2) * 512 + 512], inc="Y")
            yo = b.n("Y")
        yS = b.n("Y")
        b.wait("tensor", (("Y", yS),))
        b.wait("vector", (("Y", yS),))
        b.wait("scalar", (("Y", yS),))
    return b.n("Y")


def alloc_ffn_A_sbuf(nc, es, tag=""):
    ec = es.enter_context
    sb = {}
    sb["wqkv"] = ec(nc.sbuf_tensor("fa_wqkv" + tag, [128, CH * 1536], BF16))
    sb["wo"] = ec(nc.sbuf_tensor("fa_wo" + tag, [128, 4 * 1024], BF16))
    sb["rotm"] = ec(nc.sbuf_tensor("fa_rotm" + tag, [128, 128], BF16))
    sb["self_"] = ec(nc.sbuf_tensor("fa_sel" + tag, [128, 32], BF16))
    sb["bself"] = ec(nc.sbuf_tensor("fa_bsel" + tag, [8, 512], F32))
    sb["maskw"] = ec(nc.sbuf_tensor("fa_maskw" + tag, [128, 1408], BF16))
    sb["qscol"] = ec(nc.sbuf_tensor("fa_qscol" + tag, [128, 1], F32))
    sb["kscol"] = ec(nc.sbuf_tensor("fa_kscol" + tag, [128, 1], F32))
    sb["tabs"] = ec(nc.sbuf_tensor("fa_tabs" + tag, [128, 4 * 512], BF16))
    sb["xr"] = ec(nc.sbuf_tensor("fa_xr" + tag, [128, CH * 512], F32))
    sb["xrb"] = ec(nc.sbuf_tensor("fa_xrb" + tag, [128, CH * 512], BF16))
    sb["zin"] = ec(nc.sbuf_tensor("fa_zin" + tag, [128, CH * 513], F32))
    sb["zinb"] = ec(nc.sbuf_tensor("fa_zinb" + tag, [128, CH * 513], BF16))
    sb["qraw"] = ec(nc.sbuf_tensor("fa_qraw" + tag, [128, 4 * 512], BF16))
    sb["kraw"] = ec(nc.sbuf_tensor("fa_kraw" + tag, [128, 4 * 512], BF16))
    sb["qprime"] = ec(nc.sbuf_tensor("fa_qprime" + tag, [128, 4 * 512], BF16))
    sb["kroll2"] = ec(nc.sbuf_tensor("fa_kroll2" + tag, [128, 4 * 1024], BF16))
    sb["vtr2"] = ec(nc.sbuf_tensor("fa_vtr2" + tag, [128, 8 * 520], BF16))
    sb["sq"] = ec(nc.sbuf_tensor("fa_sq" + tag, [128, 1024], BF16))
    sb["qsr"] = ec(nc.sbuf_tensor("fa_qsr" + tag, [128, 512], BF16))
    sb["negm4"] = ec(nc.sbuf_tensor("fa_negm4" + tag, [128, 4], F32))
    sb["expf"] = ec(nc.sbuf_tensor("fa_expf" + tag, [128, 2 * 512], BF16))
    sb["usb"] = ec(nc.sbuf_tensor("fa_usb" + tag, [64, 512], BF16))
    sb["ao2"] = ec(nc.sbuf_tensor("fa_ao2" + tag, [128, 4 * 512], BF16))
    sb["zmidA"] = ec(nc.sbuf_tensor("fa_zmid" + tag, [128, CH * 512], F32))
    return sb


def alloc_ffn_B_sbuf(nc, es, tag=""):
    ec = es.enter_context
    sb = {}
    sb["w1p"] = ec(nc.sbuf_tensor("fb_w1p" + tag, [128, CH * 5632], BF16))
    sb["w2p"] = ec(nc.sbuf_tensor("fb_w2p" + tag, [128, 22 * 1024], BF16))
    sb["b1c"] = ec(nc.sbuf_tensor("fb_b1c" + tag, [128, 44], F32))
    sb["b2c"] = ec(nc.sbuf_tensor("fb_b2c" + tag, [128, 8], F32))
    sb["lgc"] = ec(nc.sbuf_tensor("fb_lgc" + tag, [128, 8], F32))
    sb["lbc"] = ec(nc.sbuf_tensor("fb_lbc" + tag, [128, 8], F32))
    sb["zm"] = ec(nc.sbuf_tensor("fb_zm" + tag, [128, CH * 513], F32))
    sb["zc"] = ec(nc.sbuf_tensor("fb_zc" + tag, [128, 2 * 513], BF16))
    sb["ln2b"] = ec(nc.sbuf_tensor("fb_ln2b" + tag, [128, CH * 512], BF16))
    sb["gel"] = ec(nc.sbuf_tensor("fb_gel" + tag, [128, 2 * 512], BF16))
    sb["hg"] = ec(nc.sbuf_tensor("fb_hg" + tag, [128, 22 * 512], BF16))

    sb["zoc"] = ec(nc.sbuf_tensor("fb_zoc" + tag, [128, 2 * 512], F32))
    return sb


def decl_ffn_io(nc, io):
    io["f_wqkv"] = nc.dram_tensor("f_wqkv", [2, D, 1536], F32, kind="ExternalInput").ap()
    io["f_wo"] = nc.dram_tensor("f_wo", [2, 512, D], F32, kind="ExternalInput").ap()
    io["f_w1p"] = nc.dram_tensor("f_w1p", [2, D, 5632], F32, kind="ExternalInput").ap()
    io["f_w2p"] = nc.dram_tensor("f_w2p", [2, 2816, D], F32, kind="ExternalInput").ap()
    for k, shp in (("f_b1p", [128, 88]), ("f_b2", [128, 16]), ("f_lg", [128, 16]),
                   ("f_lb", [128, 16]), ("f_qs", [2, 128]), ("f_ks", [2, 128]),
                   ("f_AQ", [128, T]), ("f_BQ", [128, T]), ("f_AK", [128, T]),
                   ("f_BK", [128, T]), ("f_rot", [128, 128]), ("f_sel", [128, 32]),
                   ("f_bsel", [8, 512]), ("f_maskw", [128, 1408])):
        io[k] = nc.dram_tensor(k, shp, F32, kind="ExternalInput").ap()


def build_ffn(src_name="x1", out_kind="ExternalOutput", debug=False, add_ab=False):
    nc = bass.Bass("TRN2", target_bir_lowering=False)
    io = {}
    if debug:
        DBG["taps"] = []
        DBG["off"] = 0
        io["dbg"] = nc.dram_tensor("dbg", [128, 65536], F32, kind="ExternalOutput").ap()
    io["x1"] = nc.dram_tensor("x1", [D, T], F32, kind="ExternalInput").ap()
    if add_ab:
        io["abin"] = nc.dram_tensor("abin", [D, T], F32, kind="ExternalInput").ap()
        io["x3d"] = nc.dram_tensor("x3d", [D, T], F32, kind="Internal").ap()
    io["ident"] = nc.dram_tensor("ident", [128, 128], F32, kind="ExternalInput").ap()
    decl_ffn_io(nc, io)
    sk = "ExternalOutput" if debug else "Internal"
    io["zmid"] = nc.dram_tensor("zmid", [D, T], F32, kind=sk).ap()
    io["z1"] = nc.dram_tensor("z1", [D, T], F32, kind=sk).ap()
    io["x2"] = nc.dram_tensor("x2", [D, T], F32, kind=out_kind).ap()

    with contextlib.ExitStack() as es:
        cm = alloc_common_sbuf(nc, es)
        cm["sqb"] = es.enter_context(nc.sbuf_tensor("c_sqb", [128, 2 * 512], BF16))
        ps = es.enter_context(nc.psum_tensor("PS", [128, 4096], F32))
        b = Bld(nc, es)
        for s_ in ("T", "V", "S", "G", "Y"):
            b.sem(s_)
        emit_common_init(b, io, cm)
        srcn = "x1"
        if add_ab:
            srcn = "x3d"
            with contextlib.ExitStack() as esP:
                pa = esP.enter_context(nc.sbuf_tensor("pp_a", [128, CH * 512], F32))
                pb = esP.enter_context(nc.sbuf_tensor("pp_b", [128, CH * 512], F32))
                for s in range(NSLAB):
                    qs = s * SLAB
                    for c in range(CH):
                        b.op("sync", "dma_start", out=pa[:, c * 512:(c + 1) * 512],
                             in_=io["x1"][c * 128:(c + 1) * 128, qs:qs + 512],
                             w=(("V", b.n("V")),), inc="Y")
                        b.op("sync", "dma_start", out=pb[:, c * 512:(c + 1) * 512],
                             in_=io["abin"][c * 128:(c + 1) * 128, qs:qs + 512], inc="Y")
                    yl = b.n("Y")
                    b.wait("vector", (("Y", yl),))
                    for c in range(CH):
                        b.op("vector", "tensor_tensor", pa[:, c * 512:(c + 1) * 512],
                             pa[:, c * 512:(c + 1) * 512], pb[:, c * 512:(c + 1) * 512],
                             ALU.add, inc="V")
                        vv_ = b.n("V")
                        b.wait("sync", (("V", vv_),))
                        b.op("sync", "dma_start", out=io["x3d"][c * 128:(c + 1) * 128, qs:qs + 512],
                             in_=pa[:, c * 512:(c + 1) * 512], inc="Y")
                _phase_join(b)
        with contextlib.ExitStack() as esA:
            sbA = alloc_ffn_A_sbuf(nc, esA, "0")
            emit_ffn_phaseA(b, io, sbA, cm, ps, 0, srcn, "zmid", True)
            _phase_join(b)
        with contextlib.ExitStack() as esB:
            sbB = alloc_ffn_B_sbuf(nc, esB, "0")
            emit_ffn_phaseB(b, io, sbB, cm, ps, 0, "zmid", "z1", None)
            _phase_join(b)
        with contextlib.ExitStack() as esA:
            sbA = alloc_ffn_A_sbuf(nc, esA, "1")
            emit_ffn_phaseA(b, io, sbA, cm, ps, 1, "z1", "zmid", False)
            _phase_join(b)
        with contextlib.ExitStack() as esB:
            sbB = alloc_ffn_B_sbuf(nc, esB, "1")
            yfin = emit_ffn_phaseB(b, io, sbB, cm, ps, 1, "zmid", "x2", srcn)
            b.wait("sync", (("Y", yfin),))
        blk = es.enter_context(nc.Block())
        for eng in ENGINES:
            ops = b.q[eng]
            if ops:
                getattr(blk, eng)(lambda e, ops=ops: [f(e) for f in ops])
    return nc


def _phase_join(b):
    marks = [("T", b.n("T")), ("V", b.n("V")), ("S", b.n("S")), ("G", b.n("G")), ("Y", b.n("Y"))]
    for e in ENGINES:
        b.wait(e, tuple(marks))




def decl_attn_b_io(nc, io):
    for k in ("abDirTq", "abDirTv", "abKwT", "abOutWT"):
        io[k] = nc.dram_tensor(k, [D, D], F32, kind="ExternalInput").ap()
    io["abAq"] = nc.dram_tensor("abAq", [8, D], F32, kind="ExternalInput").ap()
    io["abAv"] = nc.dram_tensor("abAv", [8, D], F32, kind="ExternalInput").ap()
    io["abATq"] = nc.dram_tensor("abATq", [D, 8], F32, kind="ExternalInput").ap()
    io["abATv"] = nc.dram_tensor("abATv", [D, 8], F32, kind="ExternalInput").ap()
    io["abBTq"] = nc.dram_tensor("abBTq", [8, D], F32, kind="ExternalInput").ap()
    io["abBTv"] = nc.dram_tensor("abBTv", [8, D], F32, kind="ExternalInput").ap()
    for k in ("abBq_row", "abBk_row", "abBv_row", "abBo_row"):
        io[k] = nc.dram_tensor(k, [1, D], F32, kind="ExternalInput").ap()
    io["abMagq"] = nc.dram_tensor("abMagq", [1, 1], F32, kind="ExternalInput").ap()
    io["abMagv"] = nc.dram_tensor("abMagv", [1, 1], F32, kind="ExternalInput").ap()
    io["vutq"] = nc.dram_tensor("vutq", [D, D], BF16, kind="Internal").ap()
    io["vutv"] = nc.dram_tensor("vutv", [D, D], BF16, kind="Internal").ap()


def alloc_attn_b1_sbuf(nc, es):
    ec = es.enter_context
    sb = {}
    sb["wdq"] = ec(nc.sbuf_tensor("ab_wdq", [128, CH * 1024], BF16))
    sb["wdv"] = ec(nc.sbuf_tensor("ab_wdv", [128, CH * 1024], BF16))
    sb["wk"] = ec(nc.sbuf_tensor("ab_wk", [128, CH * 1024], BF16))
    sb["Aq"] = ec(nc.sbuf_tensor("ab_Aq", [8, 1024], BF16))
    sb["Av"] = ec(nc.sbuf_tensor("ab_Av", [8, 1024], BF16))
    sb["ATq"] = ec(nc.sbuf_tensor("ab_ATq", [128, CH * 8], BF16))
    sb["ATv"] = ec(nc.sbuf_tensor("ab_ATv", [128, CH * 8], BF16))
    sb["BTq"] = ec(nc.sbuf_tensor("ab_BTq", [128, 1024], BF16))
    sb["BTv"] = ec(nc.sbuf_tensor("ab_BTv", [128, 1024], BF16))
    sb["biasq"] = ec(nc.sbuf_tensor("ab_biasq", [1, 1024], BF16))
    sb["biask"] = ec(nc.sbuf_tensor("ab_biask", [1, 1024], BF16))
    sb["biasv"] = ec(nc.sbuf_tensor("ab_biasv", [1, 1024], BF16))
    sb["magq"] = ec(nc.sbuf_tensor("ab_magq", [1, 1], F32))
    sb["magv"] = ec(nc.sbuf_tensor("ab_magv", [1, 1], F32))
    sb["dirf"] = ec(nc.sbuf_tensor("ab_dirf", [128, 512], F32))
    sb["vuf"] = ec(nc.sbuf_tensor("ab_vuf", [128, 512], F32))
    sb["vub"] = ec(nc.sbuf_tensor("ab_vub", [128, 512], BF16))
    sb["nacc"] = ec(nc.sbuf_tensor("ab_nacc", [128, 1], F32))
    sb["cq"] = ec(nc.sbuf_tensor("ab_cq", [128, 1], F32))
    sb["cv"] = ec(nc.sbuf_tensor("ab_cv", [128, 1], F32))
    sb["r11"] = ec(nc.sbuf_tensor("ab_r11", [1, 8], F32))
    sb["xt"] = ec(nc.sbuf_tensor("ab_xt", [128, 1024], F32))
    sb["xsq"] = ec(nc.sbuf_tensor("ab_xsq", [128, 1024], BF16))
    sb["mu"] = ec(nc.sbuf_tensor("ab_mu", [128, 4], F32))
    sb["lnb"] = ec(nc.sbuf_tensor("ab_lnb", [128, 1024], BF16))
    sb["xfm"] = ec(nc.sbuf_tensor("ab_xfm", [128, CH * T], BF16))
    sb["y1q"] = ec(nc.sbuf_tensor("ab_y1q", [128, T], BF16))
    sb["y1v"] = ec(nc.sbuf_tensor("ab_y1v", [128, T], BF16))
    sb["qkvev"] = ec(nc.sbuf_tensor("ab_qkvev", [128, 2 * 1024], BF16))
    return sb


def alloc_attn_b2_sbuf(nc, es):
    ec = es.enter_context
    sb = {}
    sb["wo"] = ec(nc.sbuf_tensor("ab_wo", [128, CH * 1024], BF16))
    sb["biaso"] = ec(nc.sbuf_tensor("ab_biaso", [1, 1024], BF16))
    sb["qt"] = ec(nc.sbuf_tensor("ab_qt", [128, 8 * 1024], BF16))
    sb["kt"] = ec(nc.sbuf_tensor("ab_kt", [128, 8 * 1024], BF16))
    sb["vt"] = ec(nc.sbuf_tensor("ab_vt", [128, 8 * 1024], BF16))
    sb["tmp"] = ec(nc.sbuf_tensor("ab_tmp", [128, 1024], F32))
    sb["oacc"] = ec(nc.sbuf_tensor("ab_oacc", [128, 1024], F32))
    sb["Ssb"] = ec(nc.sbuf_tensor("ab_Ssb", [128, 128], F32))
    sb["Esb"] = ec(nc.sbuf_tensor("ab_Esb", [128, 128], BF16))
    sb["es"] = ec(nc.sbuf_tensor("ab_es", [128, 16], F32))
    sb["res"] = ec(nc.sbuf_tensor("ab_res", [128, 16], F32))
    sb["aot"] = ec(nc.sbuf_tensor("ab_aot", [128, 1024], BF16))
    sb["aofm"] = ec(nc.sbuf_tensor("ab_aofm", [128, 8 * 128], BF16))
    sb["zo"] = ec(nc.sbuf_tensor("ab_zo", [128, 1024], F32))
    return sb


def emit_attn_b1(b: Bld, io, sb, cm, ps, src_name="abx", ln_inside=True):
    nc = b.nc
    PS = ps

    def bank(i, lo=0, n=512):
        return PS[:, 512 * i + lo: 512 * i + lo + n]

    src = io[src_name]
    lndram = io["ablnb"]

    for kc in range(CH):
        b.op("gpsimd", "dma_start", out=sb["wk"][:, kc * 1024:(kc + 1) * 1024],
             in_=io["abKwT"][kc * 128:(kc + 1) * 128, :], inc="G")
    for nm_s, nm_d in (("abAq", "Aq"), ("abAv", "Av")):
        b.op("gpsimd", "dma_start", out=sb[nm_d][:, :], in_=io[nm_s][:, :], inc="G")
    b.op("vector", "memset", sb["BTq"][:, :], 0.0)
    b.op("vector", "memset", sb["BTv"][:, :], 0.0)
    b.op("vector", "memset", sb["y1q"][:, :], 0.0)
    b.op("vector", "memset", sb["y1v"][:, :], 0.0, inc="V")
    vz0 = b.n("V")
    b.wait("gpsimd", (("V", vz0),))
    b.op("gpsimd", "dma_start", out=sb["BTq"][0:8, :], in_=io["abBTq"][:, :], inc="G")
    b.op("gpsimd", "dma_start", out=sb["BTv"][0:8, :], in_=io["abBTv"][:, :], inc="G")
    for kc in range(CH):
        b.op("gpsimd", "dma_start", out=sb["ATq"][:, kc * 8:(kc + 1) * 8],
             in_=io["abATq"][kc * 128:(kc + 1) * 128, :], inc="G")
        b.op("gpsimd", "dma_start", out=sb["ATv"][:, kc * 8:(kc + 1) * 8],
             in_=io["abATv"][kc * 128:(kc + 1) * 128, :], inc="G")
    for nm_s, nm_d in (("abBq_row", "biasq"), ("abBk_row", "biask"), ("abBv_row", "biasv")):
        b.op("gpsimd", "dma_start", out=sb[nm_d][:, :], in_=io[nm_s][:, :], inc="G")
    b.op("gpsimd", "dma_start", out=sb["magq"][:, :], in_=io["abMagq"][:, :], inc="G")
    b.op("gpsimd", "dma_start", out=sb["magv"][:, :], in_=io["abMagv"][:, :], inc="G")
    g0 = b.n("G")

    for (Amat, BT, dirio, mag, ccol, qscale, vudram) in (
        (sb["Aq"], sb["BTq"], io["abDirTq"], sb["magq"], sb["cq"], True, io["vutq"]),
        (sb["Av"], sb["BTv"], io["abDirTv"], sb["magv"], sb["cv"], False, io["vutv"]),
    ):
        b.op("vector", "memset", sb["nacc"][:, :], 0.0, w=(("G", g0),), inc="V")
        b.wait("tensor", (("G", g0),))
        for kc in range(CH):
            for ms_ in range(2):
                b.op("tensor", "matmul", bank(0),
                     Amat[:, kc * 128:(kc + 1) * 128], BT[0:8, ms_ * 512:(ms_ + 1) * 512],
                     start=True, stop=True, inc="T")
                tBA = b.n("T")
                b.op("sync", "dma_start", out=sb["dirf"][:, :],
                     in_=dirio[kc * 128:(kc + 1) * 128, ms_ * 512:(ms_ + 1) * 512],
                     w=(("V", b.n("V")), ("S", b.n("S"))), inc="Y")
                yD = b.n("Y")
                b.op("vector", "tensor_tensor", sb["vuf"][:, :], bank(0), sb["dirf"][:, :],
                     ALU.add, w=(("T", tBA), ("Y", yD)), inc="V")
                vVu = b.n("V")
                b.wait("scalar", (("V", vVu),))
                b.op("scalar", "copy", sb["vub"][:, :], sb["vuf"][:, :], inc="S")
                sVb = b.n("S")
                b.wait("sync", (("S", sVb),))
                b.op("sync", "dma_start",
                     out=vudram[kc * 128:(kc + 1) * 128, ms_ * 512:(ms_ + 1) * 512],
                     in_=sb["vub"][:, :], inc="Y")
                b.op("scalar", "activation", sb["xt"][:, 0:512], sb["vuf"][:, :], AF.Square,
                     inc="S")
                sSq = b.n("S")
                b.op("vector", "reduce_sum", sb["vuf"][:, 0:1], sb["xt"][:, 0:512],
                     axis=AX.X, w=(("S", sSq),))
                b.op("vector", "tensor_tensor", sb["nacc"][:, :], sb["nacc"][:, :],
                     sb["vuf"][:, 0:1], ALU.add, inc="V")
                vN = b.n("V")
                b.wait("tensor", (("V", vN),))
        b.op("tensor", "matmul", PS[0:1, 0:1], cm["ones_f32col"][:, :], sb["nacc"][:, :],
             start=True, stop=True, inc="T")
        tT = b.n("T")
        b.wait("scalar", (("T", tT),))
        b.op("scalar", "activation", sb["r11"][:, 0:1], PS[0:1, 0:1], AF.Sqrt,
             bias=0.0, scale=1.0, inc="S")
        sR = b.n("S")
        b.wait("vector", (("S", sR),))
        b.op("vector", "tensor_scalar_add", sb["r11"][:, 1:2], sb["r11"][:, 0:1], 1e-8)
        b.op("vector", "reciprocal", sb["r11"][:, 2:3], sb["r11"][:, 1:2])
        b.op("vector", "tensor_tensor", sb["r11"][:, 3:4], sb["r11"][:, 2:3], mag[:, :],
             ALU.mult)
        b.op("vector", "reciprocal", sb["r11"][:, 4:5], sb["r11"][:, 3:4])
        if qscale:
            b.op("vector", "tensor_scalar_mul", sb["r11"][:, 5:6], sb["r11"][:, 3:4], 0.125)
        else:
            b.op("vector", "tensor_copy", sb["r11"][:, 5:6], sb["r11"][:, 3:4])
        brow = sb["biasq"] if qscale else sb["biasv"]
        b.op("vector", "tensor_single_scalar", brow[:, :], brow[:, :], sb["r11"][:, 4:5],
             ALU.mult)
        b.op("vector", "tensor_copy", cm["tmpf"][:, 0:128], cm["identf"][:, 0:128])
        b.op("vector", "tensor_copy", cm["tmpf"][:, 0:128], cm["identf"][:, 0:128], inc="V")
        vC = b.n("V")
        b.wait("tensor", (("V", vC),))
        b.op("tensor", "matmul", PS[:, 0:1], cm["ones_f"][:, :], sb["r11"][:, 5:6],
             start=True, stop=True, inc="T")
        tC = b.n("T")
        b.wait("vector", (("T", tC),))
        b.op("vector", "tensor_copy", ccol[:, :], PS[:, 0:1])
        b.op("vector", "tensor_copy", cm["tmpf"][:, 0:128], cm["identf"][:, 0:128])
        b.op("vector", "tensor_copy", cm["tmpf"][:, 0:128], cm["identf"][:, 0:128], inc="V")
        vCC = b.n("V")
        b.wait("tensor", (("V", vCC),))
        b.wait("scalar", (("V", vCC),))
        if 11 in DBG.get("joins", ()):
            _phase_join(b)
        if io.get("dbg") is not None:
            _tap(b, io, "ab_r11_%s" % ("q" if qscale else "v"), sb["r11"][:, :], 1, 8,
                 (("V", vCC),))
            _tap(b, io, "ab_ccol_%s" % ("q" if qscale else "v"), ccol[:, :], 128, 1,
                 (("V", vCC),))
            _tap(b, io, "ab_brow_%s" % ("q" if qscale else "v"), brow[:, :], 1, 1024,
                 (("V", vCC),))

    if 10 in DBG.get("joins", ()):
        _phase_join(b)
    if ln_inside:
        prev_store = [0]
        prev_vv = [0]
        for rt in range(32):
            b.op("sync", "dma_start", out=sb["xt"][:, :],
                 in_=src[rt * 128:(rt + 1) * 128, :],
                 w=(("V", b.n("V")), ("S", b.n("S"))), inc="Y")
            yX = b.n("Y")
            b.wait("vector", (("Y", yX),))
            b.op("vector", "reduce_sum", sb["mu"][:, 0:1], sb["xt"][:, :], axis=AX.X)
            b.op("vector", "tensor_scalar_mul", sb["mu"][:, 0:1], sb["mu"][:, 0:1], 1.0 / 1024.0,
                 inc="V")
            b.wait("scalar", (("Y", yX),))
            b.op("scalar", "activation", sb["xsq"][:, :], sb["xt"][:, :], AF.Square,
                 w=(("V", prev_vv[0]),), inc="S")
            sXq = b.n("S")
            b.wait("vector", (("S", sXq),))
            b.op("vector", "reduce_sum", sb["mu"][:, 1:2], sb["xsq"][:, :], axis=AX.X)
            b.op("vector", "tensor_scalar_mul", sb["mu"][:, 1:2], sb["mu"][:, 1:2], 1.0 / 1024.0)
            b.op("vector", "tensor_tensor", sb["mu"][:, 2:3], sb["mu"][:, 0:1], sb["mu"][:, 0:1],
                 ALU.mult)
            b.op("vector", "tensor_tensor", sb["mu"][:, 1:2], sb["mu"][:, 1:2], sb["mu"][:, 2:3],
                 ALU.subtract, inc="V")
            vV = b.n("V")
            b.wait("scalar", (("V", vV),))
            b.op("scalar", "activation", sb["mu"][:, 2:3], sb["mu"][:, 1:2], AF.Sqrt,
                 bias=cm["eps_col"][:, :], scale=1.0, inc="S")
            sSd = b.n("S")
            b.wait("vector", (("S", sSd),))
            b.op("vector", "reciprocal", sb["mu"][:, 2:3], sb["mu"][:, 2:3])
            b.op("vector", "scalar_tensor_tensor", sb["mu"][:, 3:4], sb["mu"][:, 0:1], -1.0,
                 sb["mu"][:, 2:3], ALU.mult, ALU.mult, inc="V")
            vSn = b.n("V")
            b.wait("scalar", (("V", vSn),))
            b.op("scalar", "activation", sb["lnb"][:, :], sb["xt"][:, :], AF.Identity,
                 bias=sb["mu"][:, 3:4], scale=sb["mu"][:, 2:3],
                 w=(("Y", prev_store[0]),), inc="S")
            sLn = b.n("S")
            b.wait("sync", (("S", sLn),))
            b.op("sync", "dma_start", out=lndram[rt * 128:(rt + 1) * 128, :],
                 in_=sb["lnb"][:, :], inc="Y")
            prev_store[0] = b.n("Y")
            prev_vv[0] = b.n("V")
            b.wait("tensor", (("S", sLn),))
            for c_ in range(CH):
                slot = (c_ % 2) * 64
                b.op("tensor", "transpose", PS[:, 6 * 512 + slot: 6 * 512 + slot + 64].bitcast(BF16),
                     sb["lnb"][:, c_ * 128:(c_ + 1) * 128], cm["identb"][:, :], inc="T")
                tTr = b.n("T")
                b.wait("vector", (("T", tTr),))
                b.op("vector", "tensor_copy", sb["xfm"][:, c_ * T + rt * 128: c_ * T + rt * 128 + 128],
                     PS[:, 6 * 512 + slot: 6 * 512 + slot + 64].bitcast(BF16), inc="V")
                vTr = b.n("V")
                b.wait("tensor", (("V", vTr),))
            prev_vv[0] = b.n("V")
    yLN = b.n("Y")

    b.wait("gpsimd", (("Y", b.n("Y")),))
    for kc in range(CH):
        b.op("gpsimd", "dma_start", out=sb["wdq"][:, kc * 1024:(kc + 1) * 1024],
             in_=io["vutq"][kc * 128:(kc + 1) * 128, :], inc="G")
        b.op("gpsimd", "dma_start", out=sb["wdv"][:, kc * 1024:(kc + 1) * 1024],
             in_=io["vutv"][kc * 128:(kc + 1) * 128, :], inc="G")
    gVu = b.n("G")

    b.wait("tensor", (("Y", yLN), ("G", gVu)))
    for rt in range(32):
        col = rt * 128
        for (pj, (wmat, biasrow, ccol, dstd)) in enumerate((
            (sb["wdv"], sb["biasv"], sb["cv"], io["vd"]),
            (sb["wk"], sb["biask"], None, io["kd"]),
            (sb["wdq"], sb["biasq"], sb["cq"], io["qd"]),
        )):
            for ms_ in range(2):
                bk = ms_ % 2
                eh = (pj * 2 + ms_) % 2
                b.op("tensor", "matmul", bank(bk), cm["ones_b"][0:1, 0:128],
                     biasrow[:, ms_ * 512:(ms_ + 1) * 512], start=True, stop=False)
                for kc in range(CH):
                    b.op("tensor", "matmul", bank(bk),
                         sb["xfm"][:, kc * T + col: kc * T + col + 128],
                         wmat[:, kc * 1024 + ms_ * 512: kc * 1024 + ms_ * 512 + 512],
                         start=False, stop=(kc == CH - 1),
                         inc="T" if kc == CH - 1 else None)
                tP = b.n("T")
                b.wait("scalar", (("T", tP),))
                if ccol is not None:
                    b.op("scalar", "activation", sb["qkvev"][:, eh * 1024: eh * 1024 + 512],
                         bank(bk), AF.Identity, bias=0.0, scale=ccol[:, :], inc="S")
                else:
                    b.op("scalar", "copy", sb["qkvev"][:, eh * 1024: eh * 1024 + 512],
                         bank(bk), inc="S")
                sP = b.n("S")
                b.wait("sync", (("S", sP),))
                b.op("sync", "dma_start",
                     out=dstd[rt * 128:(rt + 1) * 128, ms_ * 512:(ms_ + 1) * 512],
                     in_=sb["qkvev"][:, eh * 1024: eh * 1024 + 512], inc="Y")
                yP = b.n("Y")
                b.wait("tensor", (("S", sP),))
                b.wait("scalar", (("Y", yP),))
    return b.n("Y")


def emit_attn_b2(b: Bld, io, sb, cm, ps, dst_name="abo"):
    nc = b.nc
    PS = ps

    def bank(i, lo=0, n=512):
        return PS[:, 512 * i + lo: 512 * i + lo + n]

    dst = io[dst_name]
    for kc in range(CH):
        b.op("gpsimd", "dma_start", out=sb["wo"][:, kc * 1024:(kc + 1) * 1024],
             in_=io["abOutWT"][kc * 128:(kc + 1) * 128, :], inc="G")
    b.op("gpsimd", "dma_start", out=sb["biaso"][:, :], in_=io["abBo_row"][:, :], inc="G")
    g1 = b.n("G")
    for i in range(4):
        b.wait("sync", (("V", b.n("V")), ("S", b.n("S")), ("T", b.n("T"))))
        for bb in range(8):
            r0 = bb * 512 + i * 128
            b.op("sync", "dma_start", out=sb["qt"][:, bb * 1024:(bb + 1) * 1024],
                 in_=io["qd"][r0:r0 + 128, :], inc="Y")
            b.op("sync", "dma_start", out=sb["kt"][:, bb * 1024:(bb + 1) * 1024],
                 in_=io["kd"][r0:r0 + 128, :], inc="Y")
            b.op("sync", "dma_start", out=sb["vt"][:, bb * 1024:(bb + 1) * 1024],
                 in_=io["vd"][r0:r0 + 128, :], inc="Y")
        yQ = b.n("Y")
        b.wait("vector", (("Y", yQ), ("G", g1)))
        for b1 in range(8):
            for b2 in range(8):
                b.op("vector", "tensor_tensor", sb["tmp"][:, :],
                     sb["qt"][:, b1 * 1024:(b1 + 1) * 1024],
                     sb["kt"][:, b2 * 1024:(b2 + 1) * 1024], ALU.mult)
                b.op("vector", "reduce_sum", sb["Ssb"][:, b2 * 16:(b2 + 1) * 16],
                     sb["tmp"][:, :].rearrange("p (h d) -> p h d", d=64), axis=AX.X,
                     inc="V" if b2 == 7 else None)
            vS = b.n("V")
            b.wait("scalar", (("V", vS),))
            b.op("scalar", "activation", sb["Esb"][:, :], sb["Ssb"][:, :], AF.Exp, inc="S")
            sE = b.n("S")
            b.wait("vector", (("S", sE),))
            b.op("vector", "reduce_sum", sb["es"][:, :],
                 sb["Esb"][:, :].rearrange("p (b h) -> p h b", h=16), axis=AX.X)
            b.op("vector", "reciprocal", sb["res"][:, :], sb["es"][:, :])
            for b2 in range(8):
                eap = bass.AP(sb["Esb"], b2 * 16, [[128, 128], [1, 16], [0, 64]])
                if b2 == 0:
                    b.op("vector", "tensor_tensor", sb["oacc"][:, :],
                         sb["vt"][:, b2 * 1024:(b2 + 1) * 1024], eap, ALU.mult)
                else:
                    b.op("vector", "tensor_tensor", sb["tmp"][:, :],
                         sb["vt"][:, b2 * 1024:(b2 + 1) * 1024], eap, ALU.mult)
                    b.op("vector", "tensor_tensor", sb["oacc"][:, :], sb["oacc"][:, :],
                         sb["tmp"][:, :], ALU.add)
            rap = bass.AP(sb["res"], 0, [[16, 128], [1, 16], [0, 64]])
            b.op("vector", "tensor_tensor", sb["aot"][:, :],
                 sb["oacc"][:, :], rap, ALU.mult, inc="V")
            vAo = b.n("V")
            b.wait("tensor", (("V", vAo),))
            for c_ in range(CH):
                b.op("tensor", "transpose", bank(2, 0, 64).bitcast(BF16),
                     sb["aot"][:, c_ * 128:(c_ + 1) * 128],
                     cm["identb"][:, :], inc="T")
                tTr = b.n("T")
                b.wait("vector", (("T", tTr),))
                b.op("vector", "tensor_copy", sb["aofm"][:, c_ * 128:(c_ + 1) * 128],
                     bank(2, 0, 64).bitcast(BF16), inc="V")
                vTr = b.n("V")
                b.wait("tensor", (("V", vTr),))
            for ms_ in range(2):
                bk = 3 + ms_
                for kc in range(CH):
                    b.op("tensor", "matmul", bank(bk),
                         sb["aofm"][:, kc * 128:(kc + 1) * 128],
                         sb["wo"][:, kc * 1024 + ms_ * 512: kc * 1024 + ms_ * 512 + 512],
                         start=(kc == 0), stop=False)
                b.op("tensor", "matmul", bank(bk), cm["ones_b"][0:1, 0:128],
                     sb["biaso"][:, ms_ * 512:(ms_ + 1) * 512], start=False, stop=True,
                     inc="T")
                tO = b.n("T")
                b.wait("vector", (("T", tO),))
                b.op("vector", "tensor_copy", sb["zo"][:, ms_ * 512:(ms_ + 1) * 512],
                     bank(bk), inc="V")
            vO = b.n("V")
            b.wait("sync", (("V", vO),))
            b.op("sync", "dma_start",
                 out=dst[b1 * 512 + i * 128: b1 * 512 + i * 128 + 128, :],
                 in_=sb["zo"][:, :], inc="Y")
            yO = b.n("Y")
            b.wait("vector", (("Y", yO),))
        _phase_join(b)
    return b.n("Y")


def build_attn_b(debug=False):
    nc = bass.Bass("TRN2", target_bir_lowering=False)
    io = {}
    if debug:
        DBG["taps"] = []
        DBG["off"] = 0
        io["dbg"] = nc.dram_tensor("dbg", [128, 65536], F32, kind="ExternalOutput").ap()
    io["abx"] = nc.dram_tensor("abx", [T, D], F32, kind="ExternalInput").ap()
    io["ident"] = nc.dram_tensor("ident", [128, 128], F32, kind="ExternalInput").ap()
    decl_attn_b_io(nc, io)
    io["ablnb"] = nc.dram_tensor("ablnb", [T, D], BF16, kind="Internal").ap()
    qk = "ExternalOutput"
    io["qd"] = nc.dram_tensor("qd", [T, D], BF16, kind=qk).ap()
    io["kd"] = nc.dram_tensor("kd", [T, D], BF16, kind=qk).ap()
    io["vd"] = nc.dram_tensor("vd", [T, D], BF16, kind=qk).ap()
    io["abo"] = nc.dram_tensor("abo", [T, D], F32, kind="ExternalOutput").ap()
    with contextlib.ExitStack() as es:
        cm = alloc_common_sbuf(nc, es)
        cm["sqb"] = es.enter_context(nc.sbuf_tensor("c_sqb", [128, 2 * 512], BF16))
        cm["ones_f32col"] = es.enter_context(nc.sbuf_tensor("c_onesf32c", [128, 1], F32))
        cm["identb"] = es.enter_context(nc.sbuf_tensor("c_identb", [128, 128], BF16))
        ps = es.enter_context(nc.psum_tensor("PS", [128, 4096], F32))
        b = Bld(nc, es)
        for s_ in ("T", "V", "S", "G", "Y"):
            b.sem(s_)
        emit_common_init(b, io, cm)
        b.op("vector", "memset", cm["ones_f32col"][:, :], 1.0)
        b.op("gpsimd", "dma_start", out=cm["identb"][:, :], in_=io["ident"][:, :], inc="G")
        with contextlib.ExitStack() as esB:
            sb = alloc_attn_b1_sbuf(nc, esB)
            emit_attn_b1(b, io, sb, cm, ps)
            _phase_join(b)
        with contextlib.ExitStack() as esB:
            sb = alloc_attn_b2_sbuf(nc, esB)
            yfin = emit_attn_b2(b, io, sb, cm, ps)
            b.wait("sync", (("Y", yfin),))
        blk = es.enter_context(nc.Block())
        for eng in ENGINES:
            ops = b.q[eng]
            if ops:
                getattr(blk, eng)(lambda e, ops=ops: [f(e) for f in ops])
    return nc


def host_prep_attn_b(inputs):
    f32 = np.float32
    m = {}
    m["abDirTq"] = np.ascontiguousarray(np.asarray(inputs["L0_attn_b_dir_q"], f32).T)
    m["abDirTv"] = np.ascontiguousarray(np.asarray(inputs["L0_attn_b_dir_v"], f32).T)
    m["abKwT"] = np.ascontiguousarray(np.asarray(inputs["L0_attn_b_k_w"], f32).T)
    m["abOutWT"] = np.ascontiguousarray(np.asarray(inputs["L0_attn_b_out_w"], f32).T)
    m["abAq"] = np.ascontiguousarray(np.asarray(inputs["L0_attn_b_A_q"], f32))
    m["abAv"] = np.ascontiguousarray(np.asarray(inputs["L0_attn_b_A_v"], f32))
    m["abATq"] = np.ascontiguousarray(np.asarray(inputs["L0_attn_b_A_q"], f32).T)
    m["abATv"] = np.ascontiguousarray(np.asarray(inputs["L0_attn_b_A_v"], f32).T)
    m["abBTq"] = np.ascontiguousarray(np.asarray(inputs["L0_attn_b_B_q"], f32).T)
    m["abBTv"] = np.ascontiguousarray(np.asarray(inputs["L0_attn_b_B_v"], f32).T)
    m["abBq_row"] = np.asarray(inputs["L0_attn_b_bias_q"], f32).reshape(1, D)
    m["abBk_row"] = np.asarray(inputs["L0_attn_b_k_b"], f32).reshape(1, D)
    m["abBv_row"] = np.asarray(inputs["L0_attn_b_bias_v"], f32).reshape(1, D)
    m["abBo_row"] = np.asarray(inputs["L0_attn_b_out_b"], f32).reshape(1, D)
    m["abMagq"] = np.asarray(inputs["L0_attn_b_mag_q"], f32).reshape(1, 1)
    m["abMagv"] = np.asarray(inputs["L0_attn_b_mag_v"], f32).reshape(1, 1)
    return m




def build_fused():
    from concourse.aot_env import allow_ici_all_to_all
    nc = bass.Bass("TRN2", target_bir_lowering=False)
    io = {}
    io["xin"] = nc.dram_tensor("xin", [D, T], F32, kind="ExternalInput").ap()
    io["wa"] = nc.dram_tensor("wa", [D, 3072], F32, kind="ExternalInput").ap()
    io["wp"] = nc.dram_tensor("wp", [D, D], F32, kind="ExternalInput").ap()
    io["ba_qk"] = nc.dram_tensor("ba_qk", [128, 16], F32, kind="ExternalInput").ap()
    io["ba_k_row"] = nc.dram_tensor("ba_k_row", [1, 1024], F32, kind="ExternalInput").ap()
    io["ba_v"] = nc.dram_tensor("ba_v", [1, 1024], F32, kind="ExternalInput").ap()
    io["bp_c"] = nc.dram_tensor("bp_c", [128, 8], F32, kind="ExternalInput").ap()
    for k in ("a_mask_w0", "a_mask_wm", "a_mask_lvl", "a_svec_row", "a_svec_col",
              "a_svec_row0", "a_svec_col0", "ident"):
        shp = {"a_mask_w0": [128, 64], "a_mask_wm": [128, 192], "a_mask_lvl": [120, 512],
               "a_svec_row": [1, 120], "a_svec_col": [120, 1], "a_svec_row0": [1, 120],
               "a_svec_col0": [120, 1], "ident": [128, 128]}[k]
        io[k] = nc.dram_tensor(k, shp, F32, kind="ExternalInput").ap()
    decl_ffn_io(nc, io)
    decl_attn_b_io(nc, io)
    io["x1"] = nc.dram_tensor("x1", [D, T], F32, kind="Internal").ap()
    io["zmid"] = nc.dram_tensor("zmid", [D, T], F32, kind="Internal").ap()
    io["z1"] = nc.dram_tensor("z1", [D, T], F32, kind="Internal").ap()
    io["x2"] = nc.dram_tensor("x2", [D, T], F32, kind="Internal").ap()
    io["a2a_in"] = nc.dram_tensor("a2a_in", [T, D], F32, kind="Internal").ap()
    io["abx"] = nc.dram_tensor("abx_l", [T, D], F32, kind="Internal").ap()
    io["ablnb"] = nc.dram_tensor("ablnb", [T, D], BF16, kind="Internal").ap()
    io["qd"] = nc.dram_tensor("qd", [T, D], BF16, kind="ExternalOutput").ap()
    io["kd"] = nc.dram_tensor("kd", [T, D], BF16, kind="ExternalOutput").ap()
    io["vd"] = nc.dram_tensor("vd", [T, D], BF16, kind="ExternalOutput").ap()
    io["abo"] = nc.dram_tensor("abo", [T, D], F32, kind="Internal").ap()
    io["abo_bm"] = nc.dram_tensor("abo_bm", [T, D], F32, kind="Internal").ap()
    io["x3d"] = nc.dram_tensor("x3d", [D, T], F32, kind="Internal").ap()
    io["x4"] = nc.dram_tensor("x4", [D, T], F32, kind="ExternalOutput").ap()

    with contextlib.ExitStack() as es:
        cm = alloc_common_sbuf(nc, es)
        cm["sqb"] = es.enter_context(nc.sbuf_tensor("c_sqb", [128, 2 * 512], BF16))
        cm["ones_f32col"] = es.enter_context(nc.sbuf_tensor("c_onesf32c", [128, 1], F32))
        cm["identb"] = es.enter_context(nc.sbuf_tensor("c_identb", [128, 128], BF16))
        ps = es.enter_context(nc.psum_tensor("PS", [128, 4096], F32))
        b = Bld(nc, es)
        for s_ in ("T", "V", "S", "G", "Y", "C"):
            b.sem(s_)
        emit_common_init(b, io, cm)
        b.op("vector", "memset", cm["ones_f32col"][:, :], 1.0)
        b.op("gpsimd", "dma_start", out=cm["identb"][:, :], in_=io["ident"][:, :], inc="G")

        with contextlib.ExitStack() as e1:
            sb = alloc_attn_a_sbuf(nc, e1, cm)
            b.op("gpsimd", "dma_start", out=sb["ba_vb"][:, :], in_=io["ba_v"][:, :], inc="G")
            emit_attn_a(b, io, sb, ps)
            _phase_join(b)

        with contextlib.ExitStack() as eA:
            sbA = alloc_ffn_A_sbuf(nc, eA, "0")
            emit_ffn_phaseA(b, io, sbA, cm, ps, 0, "x1", "zmid", True)
            _phase_join(b)
        with contextlib.ExitStack() as eB:
            sbB = alloc_ffn_B_sbuf(nc, eB, "0")
            emit_ffn_phaseB(b, io, sbB, cm, ps, 0, "zmid", "z1", None)
            _phase_join(b)
        with contextlib.ExitStack() as eA:
            sbA = alloc_ffn_A_sbuf(nc, eA, "1")
            emit_ffn_phaseA(b, io, sbA, cm, ps, 1, "z1", "zmid", False)
            _phase_join(b)
        with contextlib.ExitStack() as eB:
            sbB = alloc_ffn_B_sbuf(nc, eB, "1")
            emit_ffn_phaseB(b, io, sbB, cm, ps, 1, "zmid", "x2", "x1")
            _phase_join(b)

        with contextlib.ExitStack() as eT:
            tmb = eT.enter_context(nc.sbuf_tensor("rt_tmb", [128, 1024], F32))
            xc = eT.enter_context(nc.sbuf_tensor("rt_xc", [128, CH * 512], F32))
            for s in range(NSLAB):
                qs = s * SLAB
                for c in range(CH):
                    b.op("sync", "dma_start", out=xc[:, c * 512:(c + 1) * 512],
                         in_=io["x2"][c * 128:(c + 1) * 128, qs:qs + 512],
                         w=(("V", b.n("V")), ("Y", b.n("Y"))), inc="Y")
                yl = b.n("Y")
                b.wait("tensor", (("Y", yl),))
                for i in range(4):
                    for c in range(CH):
                        b.op("tensor", "transpose", ps[:, 0:128],
                             xc[:, c * 512 + i * 128: c * 512 + i * 128 + 128],
                             cm["identf"][:, :], inc="T")
                        tT2 = b.n("T")
                        b.wait("vector", (("T", tT2),))
                        b.op("vector", "tensor_copy", tmb[:, c * 128:(c + 1) * 128],
                             ps[:, 0:128], inc="V")
                        vT2 = b.n("V")
                        b.wait("tensor", (("V", vT2),))
                    b.wait("sync", (("V", b.n("V")),))
                    b.op("sync", "dma_start",
                         out=io["a2a_in"][qs + i * 128: qs + i * 128 + 128, :],
                         in_=tmb[:, :], inc="Y")
            _phase_join(b)
        with allow_ici_all_to_all():
            b.op("gpsimd", "collective_compute", "AllToAll", ALU.bypass,
                 replica_groups=[list(range(NCORE))],
                 ins=[io["a2a_in"][:, :]], outs=[io["abx"][:, :]], inc="C")
        cc1 = b.n("C")
        for e in ENGINES:
            b.wait(e, (("C", cc1),))

        with contextlib.ExitStack() as e3:
            sb = alloc_attn_b1_sbuf(nc, e3)
            emit_attn_b1(b, io, sb, cm, ps, src_name="abx")
            _phase_join(b)
        with contextlib.ExitStack() as e3:
            sb = alloc_attn_b2_sbuf(nc, e3)
            emit_attn_b2(b, io, sb, cm, ps, dst_name="abo")
            _phase_join(b)

        with allow_ici_all_to_all():
            b.op("gpsimd", "collective_compute", "AllToAll", ALU.bypass,
                 replica_groups=[list(range(NCORE))],
                 ins=[io["abo"][:, :]], outs=[io["abo_bm"][:, :]], inc="C")
        cc2 = b.n("C")
        for e in ENGINES:
            b.wait(e, (("C", cc2),))

        with contextlib.ExitStack() as eP:
            pa = eP.enter_context(nc.sbuf_tensor("pp_a", [128, CH * 512], F32))
            pt = eP.enter_context(nc.sbuf_tensor("pp_t", [128, 1024], F32))
            for s in range(NSLAB):
                qs = s * SLAB
                for c in range(CH):
                    b.op("sync", "dma_start", out=pa[:, c * 512:(c + 1) * 512],
                         in_=io["x2"][c * 128:(c + 1) * 128, qs:qs + 512],
                         w=(("V", b.n("V")),), inc="Y")
                for i in range(4):
                    b.op("sync", "dma_start", out=pt[:, :],
                         in_=io["abo_bm"][qs + i * 128: qs + i * 128 + 128, :],
                         w=(("T", b.n("T")),), inc="Y")
                    yl = b.n("Y")
                    b.wait("tensor", (("Y", yl),))
                    for c in range(CH):
                        b.op("tensor", "transpose", ps[:, 0:128],
                             pt[:, c * 128:(c + 1) * 128], cm["identf"][:, :], inc="T")
                        tT2 = b.n("T")
                        b.wait("vector", (("T", tT2),))
                        b.op("vector", "tensor_tensor",
                             pa[:, c * 512 + i * 128: c * 512 + i * 128 + 128],
                             pa[:, c * 512 + i * 128: c * 512 + i * 128 + 128],
                             ps[:, 0:128], ALU.add, inc="V")
                        vT2 = b.n("V")
                        b.wait("tensor", (("V", vT2),))
                b.wait("sync", (("V", b.n("V")),))
                for c in range(CH):
                    b.op("sync", "dma_start", out=io["x3d"][c * 128:(c + 1) * 128, qs:qs + 512],
                         in_=pa[:, c * 512:(c + 1) * 512], inc="Y")
            _phase_join(b)

        with contextlib.ExitStack() as eA:
            sbA = alloc_ffn_A_sbuf(nc, eA, "2")
            emit_ffn_phaseA(b, io, sbA, cm, ps, 0, "x3d", "zmid", True)
            _phase_join(b)
        with contextlib.ExitStack() as eB:
            sbB = alloc_ffn_B_sbuf(nc, eB, "2")
            emit_ffn_phaseB(b, io, sbB, cm, ps, 0, "zmid", "z1", None)
            _phase_join(b)
        with contextlib.ExitStack() as eA:
            sbA = alloc_ffn_A_sbuf(nc, eA, "3")
            emit_ffn_phaseA(b, io, sbA, cm, ps, 1, "z1", "zmid", False)
            _phase_join(b)
        with contextlib.ExitStack() as eB:
            sbB = alloc_ffn_B_sbuf(nc, eB, "3")
            yfin = emit_ffn_phaseB(b, io, sbB, cm, ps, 1, "zmid", "x4", "x3d")
            b.wait("sync", (("Y", yfin),))

        blk = es.enter_context(nc.Block())
        with allow_ici_all_to_all():
            for eng in ENGINES:
                ops = b.q[eng]
                if ops:
                    getattr(blk, eng)(lambda e, ops=ops: [f(e) for f in ops])
    return nc


def kernel_fused(**inputs):
    maps = host_prep(inputs)
    abm = host_prep_attn_b(inputs)
    nc = _get("fused", build_fused)
    fmaps = []
    for c in range(NCORE):
        m = dict(maps[c])
        m.update(abm)
        fmaps.append(m)
    res = run_bass_kernel_spmd(nc, fmaps, core_ids=list(range(NCORE)))
    return np.stack([np.asarray(r["x4"]).T for r in res.results])


_BUILD_CACHE = {}


def _get(name, fn):
    if name not in _BUILD_CACHE:
        _BUILD_CACHE[name] = fn()
    return _BUILD_CACHE[name]


def kernel(**inputs):
    _ = inputs.get("L0_attn_b_key"), inputs.get("L0_attn_b_value")
    maps = host_prep(inputs)
    abm = host_prep_attn_b(inputs)
    ident = host_consts()["ident"]

    nc1 = _get("attn_a", build_attn_a)
    res1 = run_bass_kernel_spmd(nc1, maps, core_ids=list(range(NCORE)))
    x1 = [np.ascontiguousarray(np.asarray(r["x1"])) for r in res1.results]

    nc2 = _get("ffn", build_ffn)
    maps2 = [dict(maps[c], x1=x1[c]) for c in range(NCORE)]
    res2 = run_bass_kernel_spmd(nc2, maps2, core_ids=list(range(NCORE)))
    x2 = [np.ascontiguousarray(np.asarray(r["x2"])) for r in res2.results]

    nc3 = _get("attn_b", build_attn_b)
    maps3 = []
    for c in range(NCORE):
        m = dict(abm)
        m["ident"] = ident
        m["abx"] = np.ascontiguousarray(
            np.concatenate([x2[bb].T[c * SLAB:(c + 1) * SLAB] for bb in range(NCORE)], axis=0))
        maps3.append(m)
    res3 = run_bass_kernel_spmd(nc3, maps3, core_ids=list(range(NCORE)))
    abo = [np.asarray(r["abo"]) for r in res3.results]
    abf = []
    for bb in range(NCORE):
        rows = np.concatenate([abo[c][bb * SLAB:(bb + 1) * SLAB] for c in range(NCORE)], axis=0)
        abf.append(np.ascontiguousarray(rows.T))

    nc4 = _get("ffn_add", lambda: build_ffn(add_ab=True))
    maps4 = [dict(maps[c], x1=x2[c], abin=abf[c]) for c in range(NCORE)]
    res4 = run_bass_kernel_spmd(nc4, maps4, core_ids=list(range(NCORE)))
    out = np.stack([np.asarray(r["x2"]).T for r in res4.results])
    return out
